# version 18
# baseline (speedup 1.0000x reference)
import numpy as np
import ml_dtypes
import concourse.bass as bass
import concourse.mybir as mybir
from concourse.bass_utils import run_bass_kernel_spmd

F32 = mybir.dt.float32
BF16 = mybir.dt.bfloat16
AX = mybir.AxisListType
ALU = mybir.AluOpType
ACT = mybir.ActivationFunctionType
BF = ml_dtypes.bfloat16

B, T, D, DFF, E = 8, 2048, 2048, 5461, 8
TILES = T // 128
KC = D // 128
ROT = 1024
NTOK = B * T
KSEL = NTOK // E
FCH = [(f * 128, min(128, DFF - f * 128)) for f in range((DFF + 127) // 128)]
NF = len(FCH)
FFN_GROUPS = [(0, 8), (8, 8)]
BIS_ITERS = 28
EPS = 1e-5


def _gumbel_table():
    def threefry(k0, k1, x0, x1):
        rot = [(13, 15, 26, 6), (17, 29, 16, 24)]
        ks = [np.uint32(k0), np.uint32(k1),
              np.uint32(k0) ^ np.uint32(k1) ^ np.uint32(0x1BD11BDA)]

        def rotl(x, r):
            r = np.uint32(r)
            return ((x << r) | (x >> (np.uint32(32) - r))).astype(np.uint32)

        with np.errstate(over="ignore"):
            x0 = (x0.astype(np.uint32) + ks[0]).astype(np.uint32)
            x1 = (x1.astype(np.uint32) + ks[1]).astype(np.uint32)
            for i in range(5):
                for r in rot[i % 2]:
                    x0 = (x0 + x1).astype(np.uint32)
                    x1 = (rotl(x1, r) ^ x0).astype(np.uint32)
                x0 = (x0 + ks[(i + 1) % 3]).astype(np.uint32)
                x1 = (x1 + ks[(i + 2) % 3] + np.uint32(i + 1)).astype(np.uint32)
        return x0, x1

    n = NTOK * E
    iota = np.arange(n, dtype=np.uint64)
    b1, b2 = threefry(0, 42, (iota >> np.uint64(32)).astype(np.uint32),
                      (iota & np.uint64(0xFFFFFFFF)).astype(np.uint32))
    bits = (b1 ^ b2).astype(np.uint32)
    u = ((bits >> np.uint32(9)) | np.uint32(0x3F800000)).view(np.float32) - np.float32(1.0)
    eps = np.float32(1e-10)
    g = -np.log((-np.log(u + eps, dtype=np.float32)) + eps, dtype=np.float32)
    return g.reshape(NTOK, E).astype(np.float32)


def _rotary_tables():
    inv_freq = 1.0 / (10000.0 ** (np.arange(0, ROT, 2, dtype=np.float64) / ROT))
    freqs = np.arange(T, dtype=np.float64)[:, None] * inv_freq[None, :]
    freqs = np.repeat(freqs, 2, axis=-1)
    return np.cos(freqs).astype(np.float32), np.sin(freqs).astype(np.float32)


class P:

    ENGS = ("sync", "gpsimd", "tensor", "vector", "scalar")

    def __init__(self):
        self.ops = {k: [] for k in self.ENGS}
        self.cnt = {}
        self.sems = {}
        self.lastwait = {k: {} for k in self.ENGS}

    def wait(self, eng, sname, val):
        if val is None or val <= 0:
            return
        if self.lastwait[eng].get(sname, -1) >= val:
            return
        self.lastwait[eng][sname] = val
        h = self.sems[sname]
        self.ops[eng].append(lambda e, h=h, v=val: e.wait_ge(h, v))

    def do(self, eng, fn):
        self.ops[eng].append(fn)

    def dr(self, eng):
        self.ops[eng].append(lambda e: e.drain())

    def inc(self, eng, fn, sname, by):
        v = self.cnt.get(sname, 0) + by
        self.cnt[sname] = v
        h = self.sems[sname]
        self.ops[eng].append(lambda e, fn=fn, h=h, by=by: fn(e).then_inc(h, by))
        return v

    def val(self, sname):
        return self.cnt.get(sname, 0)


def build_program(debug=False):
    nc = bass.Bass("TRN2", target_bir_lowering=False, detect_race_conditions=False)

    def din(name, shape, dt=F32):
        return nc.dram_tensor(name, list(shape), dt, kind="ExternalInput").ap()

    x_h = din("x", (T, D))
    win_h = din("win", (D, 3 * D))
    bin_h = din("bin", (3 * D,))
    wout_h = din("wout", (D, D))
    bout_h = din("bout", (D,))
    gam_h = din("gamma", (D,))
    bet_h = din("beta", (D,))
    rw_h = din("rw", (D, E))
    rb_h = din("rb", (E,))
    gu_h = din("gu_img", (NF, 128, KC * 256))
    bg_h = din("bg", (DFF,))
    bu_h = din("bu", (DFF,))
    wdi_h = din("wd_img", (16, 128, NF * 128))
    bd_h = din("bd", (D,))
    cos_h = din("cost", (T, ROT))
    sin_h = din("sint", (T, ROT))
    gum_h = din("gumb", (T, E))
    idb_h = din("identb", (128, 128), BF16)
    idf_h = din("identf", (128, 128), F32)
    cst_h = din("cvec", (4,))

    y_h = nc.dram_tensor("y", [T, D], F32, kind="ExternalOutput").ap()
    x2sp = nc.dram_tensor("x2spill", [T, D], F32, kind="Internal").ap()
    tau_d = nc.dram_tensor("tau_d", [1, E], F32, kind="Internal").ap()
    cc_in = nc.dram_tensor("cc_in", [E, T], F32, kind="Internal")
    cc_out = nc.dram_tensor("cc_out", [B * E, T], F32, kind="Internal")

    dbg = {}
    if debug:
        for nm, shape in [("d_at", (128, KC)), ("d_sm", (128, TILES * E)),
                          ("d_tau", (128, E)), ("d_scol", (128, TILES)),
                          ("d_invn", (128, TILES)), ("d_x2", (T, D)),
                          ("d_lnt", (128, T)), ("d_lnt2", (128, T)),
                          ("d_x2t", (128, D)), ("d_xnb0", (128, D))]:
            dbg[nm] = nc.dram_tensor(nm, list(shape), F32, kind="ExternalOutput").ap()

    em = P()
    ctxs = []

    def sb(name, shape, dt):
        cm = nc.sbuf_tensor(name, list(shape), dt)
        h = cm.__enter__()
        ctxs.append(cm)
        return h

    def sb_scoped(name, shape, dt, bag):
        cm = nc.sbuf_tensor(name, list(shape), dt)
        h = cm.__enter__()
        bag.append(cm)
        return h

    def psum(name, shape, dt=F32):
        cm = nc.psum_tensor(name, list(shape), dt)
        h = cm.__enter__()
        ctxs.append(cm)
        return h

    for sname in ("ld", "st", "w", "gst", "g", "mm", "v", "sc"):
        cm = nc.semaphore("s_" + sname)
        em.sems[sname] = cm.__enter__()
        ctxs.append(cm)

    XA = sb("XA", [128, KC * T], BF16)
    XBW = KC * T
    XB = sb("XB", [128, XBW], BF16)
    smloc = sb("smloc", [128, TILES * E], F32)
    scol = sb("scol", [128, TILES], F32)
    taub = sb("taub", [128, E], F32)
    identb = sb("identbs", [128, 128], BF16)
    identf = sb("identfs", [128, 128], F32)
    bgpp = sb("bgpp", [128, NF], F32)
    bupp = sb("bupp", [128, NF], F32)
    bdcols = sb("bdcols", [128, 16], F32)
    r8b = sb("r8b", [128, E], F32)
    bst = sb("bst", [128, 8], F32)

    ps = [psum(f"ps{i}", [128, 512]) for i in range(6)]
    psTp = [psum("psTa", [128, 1024], BF16), psum("psTb", [128, 1024], BF16)]

    def ld(dst, src, eng="sync"):
        sname = "ld" if eng == "sync" else "w"
        return em.inc(eng, lambda e, d=dst, s=src: e.dma_start(out=d, in_=s), sname, 16)

    def st(dst, src, eng="sync"):
        sname = "st" if eng == "sync" else "gst"
        return em.inc(eng, lambda e, d=dst, s=src: e.dma_start(out=d, in_=s), sname, 16)

    cst3 = sb("cst3", [128, 4], F32)
    g4cm = []
    stg = sb_scoped("stg", [128, 128], F32, g4cm)
    gum_sb = sb_scoped("gum_sb", [128, TILES * E], F32, g4cm)
    rwb = sb_scoped("rwb", [128, KC * E], BF16, g4cm)
    rbb = sb_scoped("rbb", [128, E], F32, g4cm)
    r8a = sb_scoped("r8a", [128, E], F32, g4cm)
    mu1 = sb_scoped("mu1", [128, TILES], F32, g4cm)
    rstd1 = sb_scoped("rstd1", [128, TILES], F32, g4cm)
    c1t = sb_scoped("c1t", [128, TILES], F32, g4cm)
    c2t = sb_scoped("c2t", [128, TILES], F32, g4cm)
    nk2 = sb_scoped("nk2", [128, TILES * 4], F32, g4cm)
    nv2 = sb_scoped("nv2", [128, TILES * 4], F32, g4cm)
    nk2r = sb_scoped("nk2r", [128, TILES], F32, g4cm)
    nv2r = sb_scoped("nv2r", [128, TILES], F32, g4cm)
    invnb = sb_scoped("invnb", [128, TILES], BF16, g4cm)
    invnf = sb_scoped("invnf", [128, TILES], F32, g4cm)
    at_s = sb_scoped("at_s", [128, KC], F32, g4cm)
    binq = sb_scoped("binq", [128, KC], F32, g4cm)
    csc = sb_scoped("csc", [128, 16], F32, g4cm)
    betc = sb_scoped("betc", [128, KC], F32, g4cm)
    nc.const_aps.register(F32, 0.0, cst3[:, 0:1])
    nc.const_aps.register(F32, 0.5, cst3[:, 1:2])
    nc.const_aps.register(F32, EPS, cst3[:, 2:3])
    nc.const_aps.register(F32, 1.0, cst3[:, 3:4])
    ld(cst3[:], cst_h[None, :].broadcast_to((128, 4)))
    ld(rbb[:], rb_h[None, :].broadcast_to((128, E)))
    ld(identb[:], idb_h)
    ld(identf[:], idf_h)
    ld(gum_sb[:].rearrange("p (i e) -> p i e", e=E), gum_h.rearrange("(i p) e -> p i e", p=128))
    ld(bst[:, 0:2], cst_h[None, 0:2].broadcast_to((128, 2)))
    ld(bgpp[0:85, 42:43], bg_h[5376:DFF, None])
    ld(bupp[0:85, 42:43], bu_h[5376:DFF, None])
    v_rwb = ld(rwb[:].rearrange("p (k e) -> p k e", e=E),
               rw_h.rearrange("(k p) e -> p k e", p=128), eng="gpsimd")
    stp_rel = 0
    for dst, src_ap, rows in ((binq, bin_h[0:D], 16), (bgpp, bg_h[0:5376], 42),
                              (bupp, bu_h[0:5376], 42), (bdcols, bd_h[0:D], 16),
                              (betc, bet_h[0:D], 16)):
        em.wait("sync", "sc", stp_rel)
        v_stg = ld(stg[0:rows, :], src_ap.rearrange("(q p) -> q p", p=128))
        em.wait("tensor", "ld", v_stg)
        v_tp0 = em.inc("tensor", lambda t, rows=rows:
                       t.transpose(ps[0][:, 0:rows], stg[0:rows, :], identf[0:rows, 0:rows]), "mm", 1)
        em.wait("scalar", "mm", v_tp0)
        stp_rel = em.inc("scalar", lambda s, dst=dst, rows=rows:
                         s.copy(dst[:, 0:rows], ps[0][:, 0:rows]), "sc", 1)
    setup_sc = em.val("sc")

    WO_SPL = 10
    wob0cm = []
    wob0 = sb_scoped("wob0", [128, WO_SPL * 512], BF16, wob0cm)
    wkv0cm = []
    wkv0 = sb_scoped("wkv0", [128, KC * 512], BF16, wkv0cm)

    p1cm = []
    xb3 = [sb_scoped(f"xb3_{j}", [128, D], F32, p1cm) for j in range(3)]
    gamt = sb_scoped("gamt", [128, D], BF16, p1cm)
    sqdump = sb_scoped("sqdump", [128, D], F32, p1cm)
    xnb = [sb_scoped("xnb0", [128, D], BF16, p1cm),
           sb_scoped("xnb1", [128, D], BF16, p1cm)]
    csc2 = sb_scoped("csc2", [128, 36], F32, p1cm)

    ld(gamt[:], gam_h[None, :].broadcast_to((128, D)), eng="gpsimd")
    v_gb = v_wk0 = ld(wkv0[:].rearrange("p (k c) -> p k c", c=512),
                      win_h[:, D:D + 512].rearrange("(k p) c -> p k c", p=128),
                      eng="gpsimd")

    tp_evac = {0: 0, 1: 0}
    xnb_rel = {}
    xb_rel = {0: 0, 1: 0, 2: 0}
    v_x = {}
    s0v = {}
    s1s = {}
    s2v = {}
    s3s = {}
    v_xnd = {}
    tpn = 0

    def cc(t, k):
        g = (t % 3) * 12
        return csc2[:, g + k:g + k + 1]

    def s0(t):
        em.wait("vector", "ld", v_x[t])
        em.wait("vector", "w", v_gb)
        em.do("vector", lambda v, t=t: v.reduce_sum(cc(t, 0), xb3[t % 3][:], axis=AX.X))
        em.dr("vector")
        em.do("vector", lambda v, t=t: v.tensor_scalar_mul(mu1[:, t:t + 1], cc(t, 0), 1.0 / D))
        s0v[t] = em.inc("vector", lambda v, t=t:
                        v.tensor_scalar_mul(cc(t, 2), cc(t, 0), -1.0 / D), "v", 1)

    def s1(t):
        em.wait("scalar", "v", s0v[t])
        em.do("scalar", lambda s, t=t: s.activation(sqdump[:], xb3[t % 3][:], ACT.Square,
                                                    bias=cc(t, 2), accum_out=cc(t, 3)))
        em.dr("scalar")
        s1s[t] = em.inc("scalar", lambda s, t=t: s.activation(cc(t, 4), cc(t, 3), ACT.Sqrt,
                                                              bias=EPS, scale=1.0 / D), "sc", 1)

    def s2(t):
        em.wait("vector", "sc", s1s[t])
        em.do("vector", lambda v, t=t: v.reciprocal(rstd1[:, t:t + 1], cc(t, 4)))
        em.dr("vector")
        em.do("vector", lambda v, t=t: v.tensor_scalar_add(c1t[:, t:t + 1], rstd1[:, t:t + 1], 1.0))
        em.do("vector", lambda v, t=t: v.tensor_mul(c2t[:, t:t + 1], mu1[:, t:t + 1],
                                                    rstd1[:, t:t + 1]))
        em.do("vector", lambda v, t=t: v.tensor_mul(cc(t, 5), rstd1[:, t:t + 1],
                                                    rstd1[:, t:t + 1]))
        em.dr("vector")
        s2v[t] = em.inc("vector", lambda v, t=t: v.tensor_mul(cc(t, 6), cc(t, 3), cc(t, 5)),
                        "v", 1)

    def s3(t):
        em.wait("scalar", "v", s2v[t])
        s3s[t] = em.inc("scalar", lambda s, t=t: s.activation(cc(t, 9), cc(t, 6), ACT.Sqrt,
                                                              bias=EPS, scale=1.0 / D), "sc", 1)

    def s4(t):
        par = t % 2
        em.wait("vector", "sc", s3s[t])
        em.wait("vector", "mm", xnb_rel.get(par, 0))
        em.do("vector", lambda v, t=t: v.reciprocal(cc(t, 10), cc(t, 9)))
        em.dr("vector")
        em.do("vector", lambda v, t=t: v.tensor_mul(cc(t, 11), cc(t, 10), rstd1[:, t:t + 1]))
        em.dr("vector")
        em.do("vector", lambda v, t=t: v.tensor_scalar(xb3[t % 3][:], xb3[t % 3][:],
                                                       mu1[:, t:t + 1], cc(t, 11),
                                                       ALU.subtract, ALU.mult))
        em.dr("vector")
        v_xn = em.inc("vector", lambda v, t=t, p=par: v.tensor_mul(xnb[p][:], xb3[t % 3][:],
                                                                   gamt[:]), "v", 1)
        xb_rel[t % 3] = v_xn
        v_xnd[t] = v_xn

    def s5(t):
        nonlocal_tpn = [0]
        par = t % 2
        em.wait("tensor", "v", v_xnd[t])
        for kc in range(KC):
            slot = (t * KC + kc) % 2
            em.wait("tensor", "sc", tp_evac[slot])
            v_tp = em.inc("tensor", lambda tt, p=par, kc=kc, sl=slot:
                          tt.transpose(psTp[sl][:, 0:128], xnb[p][:, kc * 128:(kc + 1) * 128],
                                       identb[:]), "mm", 1)
            if kc == KC - 1:
                xnb_rel[par] = v_tp
            em.wait("scalar", "mm", v_tp)
            tp_evac[slot] = em.inc("scalar", lambda s, t=t, kc=kc, sl=slot:
                                   s.activation(XA[:, kc * T + t * 128: kc * T + (t + 1) * 128],
                                                psTp[sl][:, 0:128], ACT.Identity,
                                                bias=betc[:, kc:kc + 1]), "sc", 1)

    for t in range(TILES + 2):
        if t < TILES:
            em.wait("sync", "v", xb_rel[t % 3])
            v_x[t] = ld(xb3[t % 3][:], x_h[t * 128:(t + 1) * 128, :])
            s0(t)
            s1(t)
        if 1 <= t <= TILES and t - 1 < TILES:
            s2(t - 1)
            s3(t - 1)
        if t >= 2:
            s4(t - 2)
            s5(t - 2)

    p1_mm = em.val("mm")
    p1_sc = em.val("sc")
    p1_v = em.val("v")
    for cm in reversed(p1cm):
        cm.__exit__(None, None, None)

    p2cm = []
    wkv = [wkv0, sb_scoped("wkv1", [128, KC * 512], BF16, p2cm)]
    cstl = [sb_scoped("cst0", [128, 512], F32, p2cm),
            sb_scoped("cst1", [128, 512], F32, p2cm)]
    sntl = [sb_scoped("snt0", [128, 512], F32, p2cm),
            sb_scoped("snt1", [128, 512], F32, p2cm)]
    bkt = [sb_scoped("bkt0", [128, 512], F32, p2cm),
           sb_scoped("bkt1", [128, 512], F32, p2cm)]
    kbuf2 = [sb_scoped("kbuf0", [128, 512], F32, p2cm),
             sb_scoped("kbuf1", [128, 512], F32, p2cm)]
    rs1 = sb_scoped("rs1", [128, 256], F32, p2cm)
    rs2 = sb_scoped("rs2", [128, 256], F32, p2cm)
    vrb = sb_scoped("vrb", [128, 512], F32, p2cm)
    sq5 = sb_scoped("sq5", [128, 512], F32, p2cm)

    em.wait("gpsimd", "mm", p1_mm)
    em.wait("gpsimd", "sc", p1_sc)
    em.wait("gpsimd", "v", p1_v)
    em.wait("sync", "sc", p1_sc)
    em.wait("sync", "v", p1_v)

    wseq = [("K", n) for n in range(4)] + [("V", n) for n in range(4)]
    v_wkv = {}
    wkv_rel = {}

    def issue_wkv(idx):
        kind, n = wseq[idx]
        colbase = D + (0 if kind == "K" else D) + n * 512
        em.wait("gpsimd", "mm", wkv_rel.get(idx - 2, 0))
        v_wkv[idx] = ld(wkv[idx % 2][:].rearrange("p (k c) -> p k c", c=512),
                        win_h[:, colbase:colbase + 512].rearrange("(k p) c -> p k c", p=128),
                        eng="gpsimd")

    v_wkv[0] = v_wk0
    issue_wkv(1)
    v_wo0_pref = ld(wob0[:].rearrange("p (k c) -> p k c", c=512),
                    wout_h[0:WO_SPL * 128, 0:512].rearrange("(k p) c -> p k c", p=128), eng="gpsimd")

    psum_rel = {4: 0, 5: 0}
    sq_rel2 = {}
    cs_rel = {0: 0, 1: 0}
    bkt_rel = {0: 0, 1: 0}

    for idx, (kind, n) in enumerate(wseq):
        colbase = D + (0 if kind == "K" else D) + n * 512
        em.wait("sync", "v", bkt_rel[n % 2])
        v_bk = ld(bkt[n % 2][:], bin_h[None, colbase:colbase + 512].broadcast_to((128, 512)))
        for i in range(TILES):
            v_cs = 0
            if n < 2:
                em.wait("sync", "v", cs_rel[i % 2])
                ld(cstl[i % 2][:], cos_h[i * 128:(i + 1) * 128, n * 512:(n + 1) * 512])
                v_cs = ld(sntl[i % 2][:], sin_h[i * 128:(i + 1) * 128, n * 512:(n + 1) * 512])
            bank = 4 + (idx * TILES + i) % 2
            em.wait("tensor", "w", v_wkv[idx])
            em.wait("tensor", "sc", p1_sc)
            em.wait("tensor", "v", psum_rel[bank])
            v_mm = None
            for kc in range(KC):
                fn = (lambda t, kc=kc, i=i, b=bank, p=idx % 2:
                      t.matmul(ps[b][:],
                               XA[:, kc * T + i * 128: kc * T + (i + 1) * 128],
                               wkv[p][:, kc * 512:(kc + 1) * 512],
                               start=(kc == 0), stop=(kc == KC - 1)))
                if kc == KC - 1:
                    v_mm = em.inc("tensor", fn, "mm", 1)
                else:
                    em.do("tensor", fn)
            if i == TILES - 1:
                wkv_rel[idx] = v_mm
            kpar = (idx * TILES + i) % 2
            kbuf = kbuf2[kpar]
            em.wait("vector", "mm", v_mm)
            em.wait("vector", "ld", v_bk)
            em.wait("vector", "sc", sq_rel2.get(kpar, 0))
            v_ev = em.inc("vector", lambda v, b=bank, p=n % 2, kb=kbuf:
                          v.tensor_add(kb[:], ps[b][:], bkt[p][:]), "v", 1)
            psum_rel[bank] = v_ev
            if i == TILES - 1:
                bkt_rel[n % 2] = v_ev
            nrm = nk2 if kind == "K" else nv2
            em.wait("scalar", "v", v_ev)
            sq_rel2[kpar] = em.inc("scalar", lambda s, nrm=nrm, i=i, n=n, kb=kbuf:
                                   s.activation(sq5[:], kb[:], ACT.Square,
                                                accum_out=nrm[:, i * 4 + n: i * 4 + n + 1]), "sc", 1)
            base = i * D + n * 512
            if n < 2:
                em.wait("vector", "ld", v_cs)
                ke, ko = kbuf[:, 0:512:2], kbuf[:, 1:512:2]
                ce, co = cstl[i % 2][:, 0:512:2], cstl[i % 2][:, 1:512:2]
                se, so = sntl[i % 2][:, 0:512:2], sntl[i % 2][:, 1:512:2]
                if kind == "K":
                    de = XB[:, base:base + 512:2]
                    do = XB[:, base + 1:base + 512:2]
                else:
                    de = vrb[:, 0:512:2]
                    do = vrb[:, 1:512:2]
                em.dr("vector")
                em.do("vector", lambda v, ke=ke, ce=ce: v.tensor_mul(rs1[:], ke, ce))
                em.do("vector", lambda v, ko=ko, se=se: v.tensor_mul(rs2[:], ko, se))
                em.dr("vector")
                em.do("vector", lambda v, de=de: v.tensor_sub(de, rs1[:], rs2[:]))
                em.do("vector", lambda v, ko=ko, co=co: v.tensor_mul(rs1[:], ko, co))
                em.do("vector", lambda v, ke=ke, so=so: v.tensor_mul(rs2[:], ke, so))
                em.dr("vector")
                vr = em.inc("vector", lambda v, do=do: v.tensor_add(do, rs1[:], rs2[:]), "v", 1)
                cs_rel[i % 2] = vr
                if kind == "V":
                    em.dr("vector")
                    em.inc("vector", lambda v, base=base:
                           v.tensor_mul(XB[:, base:base + 512], XB[:, base:base + 512],
                                        vrb[:]), "v", 1)
            else:
                em.dr("vector")
                if kind == "K":
                    em.inc("vector", lambda v, base=base:
                           v.tensor_copy(XB[:, base:base + 512], kbuf[:]), "v", 1)
                else:
                    em.inc("vector", lambda v, base=base:
                           v.tensor_mul(XB[:, base:base + 512], XB[:, base:base + 512],
                                        kbuf[:]), "v", 1)
        if idx + 2 < len(wseq):
            issue_wkv(idx + 2)

    em.wait("vector", "sc", max(sq_rel2.get(0, 0), sq_rel2.get(1, 0)))
    em.do("vector", lambda v: v.reduce_sum(nk2r[:], nk2[:, :].rearrange("p (i n) -> p i n", n=4),
                                           axis=AX.X))
    em.do("vector", lambda v: v.reduce_sum(nv2r[:], nv2[:, :].rearrange("p (i n) -> p i n", n=4),
                                           axis=AX.X))
    em.dr("vector")
    em.do("vector", lambda v: v.tensor_mul(nk2r[:], nk2r[:], nv2r[:]))
    em.dr("vector")
    vv = em.inc("vector", lambda v: v.reciprocal(nv2r[:], nk2r[:]), "v", 1)
    em.wait("scalar", "v", vv)
    sv = em.inc("scalar", lambda s: s.activation(invnf[:], nv2r[:], ACT.Sqrt), "sc", 1)
    em.wait("vector", "sc", sv)
    v_invn = em.inc("vector", lambda v: v.tensor_copy(invnb[:], invnf[:]), "v", 1)

    em.wait("tensor", "v", v_invn)
    em.wait("tensor", "v", psum_rel[4])
    v_amm = None
    for dc in range(KC):
        for i in range(TILES):
            fn = (lambda t, i=i, dc=dc:
                  t.matmul(ps[4][:, dc:dc + 1], XB[:, i * D + dc * 128:i * D + (dc + 1) * 128],
                           invnb[:, i:i + 1],
                           start=(i == 0), stop=(i == TILES - 1)))
            if i == TILES - 1 and dc == KC - 1:
                v_amm = em.inc("tensor", fn, "mm", 1)
            else:
                em.do("tensor", fn)
    em.wait("scalar", "mm", v_amm)
    v_at = em.inc("scalar", lambda s: s.activation(at_s[:], ps[4][:, 0:16], ACT.Copy,
                                                   bias=0.5, scale=0.5), "sc", 1)
    if debug:
        em.wait("sync", "sc", v_at)
        st(dbg["d_at"], at_s[:])
        em.wait("sync", "v", v_invn)
        st(dbg["d_invn"], invnf[:])

    p2_mm = em.val("mm")
    p2_v = em.val("v")
    p2_sc = em.val("sc")
    for cm in reversed(p2cm):
        cm.__exit__(None, None, None)
    for cm in reversed(wkv0cm):
        cm.__exit__(None, None, None)

    p3cm = []
    wqb = [sb_scoped("wqb0", [128, KC * 128], BF16, p3cm),
           sb_scoped("wqb1", [128, KC * 128], BF16, p3cm)]

    em.wait("gpsimd", "mm", p2_mm)
    em.wait("gpsimd", "v", p2_v)
    em.wait("gpsimd", "sc", p2_sc)

    v_wq = {}
    wq_rel = {}

    def issue_wq(qc):
        em.wait("gpsimd", "mm", wq_rel.get(qc - 2, 0))
        v_wq[qc] = ld(wqb[qc % 2][:].rearrange("p (k c) -> p k c", c=128),
                      win_h[:, qc * 128:(qc + 1) * 128].rearrange("(k p) c -> p k c", p=128),
                      eng="gpsimd")

    issue_wq(0)
    issue_wq(1)
    q_ps_rel = 0
    for qc in range(KC):
        em.wait("tensor", "w", v_wq[qc])
        em.wait("tensor", "sc", v_at)
        em.wait("tensor", "v", q_ps_rel)
        v_qmm = None
        for kc in range(KC):
            for tc in range(4):
                fn = (lambda t, qc=qc, kc=kc, tc=tc:
                      t.matmul(ps[tc][:], wqb[qc % 2][:, kc * 128:(kc + 1) * 128],
                               XA[:, kc * T + tc * 512: kc * T + (tc + 1) * 512],
                               start=(kc == 0), stop=(kc == KC - 1)))
                if kc == KC - 1 and tc == 3:
                    v_qmm = em.inc("tensor", fn, "mm", 1)
                else:
                    em.do("tensor", fn)
        wq_rel[qc] = v_qmm
        em.wait("vector", "mm", v_qmm)
        for tc in range(4):
            fn = (lambda v, qc=qc, tc=tc:
                  v.tensor_scalar(XB[:, qc * T + tc * 512: qc * T + (tc + 1) * 512],
                                  ps[tc][:], binq[:, qc:qc + 1], at_s[:, qc:qc + 1],
                                  ALU.add, ALU.mult))
            if tc == 3:
                q_ps_rel = em.inc("vector", fn, "v", 1)
            else:
                em.do("vector", fn)
        if qc + 2 < KC:
            issue_wq(qc + 2)

    p3_mm = em.val("mm")
    p3_v = em.val("v")
    for cm in reversed(p3cm):
        cm.__exit__(None, None, None)

    p4cm = []
    wob0b = sb_scoped("wob0b", [128, (KC - WO_SPL) * 512], BF16, p4cm)
    wob = [None, sb_scoped("wob1", [128, KC * 512], BF16, p4cm)]

    def wo0_slice(kc):
        buf, o = (wob0, kc) if kc < WO_SPL else (wob0b, kc - WO_SPL)
        return buf[:, o * 512:(o + 1) * 512]
    bob2 = sb_scoped("bob2", [128, 1024], F32, p4cm)
    xr2 = [sb_scoped("xr20", [128, 1024], F32, p4cm),
           sb_scoped("xr21", [128, 1024], F32, p4cm)]
    x2h = [sb_scoped("x2h0", [128, 1024], F32, p4cm),
           sb_scoped("x2h1", [128, 1024], F32, p4cm)]
    x2tl = [sb_scoped("x2tl0", [128, 1024], F32, p4cm),
            sb_scoped("x2tl1", [128, 1024], F32, p4cm)]
    xnb2_s = sb_scoped("xnb20", [128, D], BF16, p4cm)
    xnb2 = [xnb2_s, xnb2_s]
    stt4 = sb_scoped("stt4", [128, 24], F32, p4cm)

    em.wait("gpsimd", "mm", p3_mm)
    em.wait("gpsimd", "v", p3_v)
    em.wait("sync", "v", p3_v)

    o_ps_rel = dict(psum_rel)
    o_ps_rel[2] = p3_v
    o_ps_rel[3] = p3_v
    bob_rel = 0
    xr_rel = {0: 0, 1: 0}
    x2h_rel = {0: 0, 1: 0}
    x2st = {}
    x2tl_ld = {}
    x2tl_rel = {0: 0, 1: 0}
    tp4_evac = dict(tp_evac)
    xnb2_rel = {}
    rt_v_rel = [p3_v]
    smt_rel = [p3_v]
    smloc_done = {}
    v_zd = {}
    tpn4 = [0]
    wo_rel = 0
    v_wo = {}

    def p4b_ln(i):
        par = i % 2
        em.wait("vector", "ld", x2tl_ld[i])
        low = x2tl[par]
        high = x2h[par]
        em.dr("vector")
        for q in range(2):
            em.do("vector", lambda v, q=q: v.bn_stats(stt4[:, q * 6:(q + 1) * 6],
                                                      low[:, q * 512:(q + 1) * 512]))
        for q in range(2):
            em.do("vector", lambda v, q=q: v.bn_stats(stt4[:, 12 + q * 6:18 + q * 6],
                                                      high[:, q * 512:(q + 1) * 512]))
        em.dr("vector")
        va = em.inc("vector", lambda v: v.bn_aggr(csc[:, 0:2], stt4[:]), "v", 1)
        em.wait("scalar", "v", va)
        em.dr("scalar")
        em.do("scalar", lambda s: s.activation(csc[:, 3:4], csc[:, 1:2], ACT.Ln, bias=EPS))
        em.dr("scalar")
        sa = em.inc("scalar", lambda s: s.activation(csc[:, 4:5], csc[:, 3:4], ACT.Exp,
                                                     scale=-0.5), "sc", 1)
        em.wait("vector", "sc", sa)
        em.wait("vector", "mm", max(xnb2_rel.get(0, 0), xnb2_rel.get(1, 0)))
        em.dr("vector")
        em.do("vector", lambda v, p=par: v.tensor_scalar(xnb2[p][:, 0:1024], low[:],
                                                         csc[:, 0:1], csc[:, 4:5],
                                                         ALU.subtract, ALU.mult))
        v_z = em.inc("vector", lambda v, p=par:
                     v.tensor_scalar(xnb2[p][:, 1024:2048], high[:], csc[:, 0:1],
                                     csc[:, 4:5], ALU.subtract, ALU.mult), "v", 1)
        x2tl_rel[par] = v_z
        v_zd[i] = v_z

    def p4b_tp(i):
        par = i % 2
        em.wait("tensor", "v", v_zd[i])
        for kc in range(KC):
            slot = tpn4[0] % 2
            em.wait("tensor", "sc", tp4_evac[slot])
            v_tp = em.inc("tensor", lambda t, p=par, kc=kc, sl=slot:
                          t.transpose(psTp[sl][:, 0:128], xnb2[p][:, kc * 128:(kc + 1) * 128],
                                      identb[:]), "mm", 1)
            if kc == KC - 1:
                xnb2_rel[par] = v_tp
            em.wait("scalar", "mm", v_tp)
            tp4_evac[slot] = em.inc("scalar", lambda s, i=i, kc=kc, sl=slot:
                                    s.copy(XA[:, kc * T + i * 128: kc * T + (i + 1) * 128],
                                           psTp[sl][:, 0:128]), "sc", 1)
            tpn4[0] += 1
        em.dr("scalar")
        tile_ln2_done = em.inc("scalar", lambda s: s.nop(), "sc", 1)
        em.wait("tensor", "sc", tile_ln2_done)
        em.wait("tensor", "sc", v_at)
        em.wait("tensor", "w", v_rwb)
        em.wait("tensor", "v", rt_v_rel[0])
        v_rt = None
        for kc in range(KC):
            fn = (lambda t, kc=kc, i=i:
                  t.matmul(ps[0][:, 0:8], XA[:, kc * T + i * 128: kc * T + (i + 1) * 128],
                           rwb[:, kc * E:(kc + 1) * E],
                           start=(kc == 0), stop=(kc == KC - 1)))
            if kc == KC - 1:
                v_rt = em.inc("tensor", fn, "mm", 1)
            else:
                em.do("tensor", fn)
        em.wait("vector", "mm", v_rt)
        v_rel = em.inc("vector", lambda v: v.tensor_add(r8a[:], ps[0][:, 0:8], rbb[:]), "v", 1)
        rt_v_rel[0] = v_rel
        em.dr("vector")
        smloc_done[i] = em.inc("vector", lambda v, i=i:
                               v.tensor_add(smloc[:, i * E:(i + 1) * E], r8a[:],
                                            gum_sb[:, i * E:(i + 1) * E]), "v", 1)

    for h in range(2):
        em.wait("gpsimd", "mm", wo_rel)
        for j in range(2):
            n = 2 * h + j
            if j == 0:
                if h == 0:
                    vA = v_wo0_pref
                else:
                    vA = ld(wob0[:].rearrange("p (k c) -> p k c", c=512),
                            wout_h[0:WO_SPL * 128, n * 512:(n + 1) * 512]
                            .rearrange("(k p) c -> p k c", p=128), eng="gpsimd")
                vB = ld(wob0b[:].rearrange("p (k c) -> p k c", c=512),
                        wout_h[WO_SPL * 128:2048, n * 512:(n + 1) * 512]
                        .rearrange("(k p) c -> p k c", p=128), eng="gpsimd")
                v_wo[0] = max(vA, vB)
                continue
            vv = 0
            for q in range(2):
                vv = ld(wob[j][q * 64:(q + 1) * 64, :].rearrange("p (k c) -> p k c", c=512),
                        wout_h[:, n * 512:(n + 1) * 512]
                        .rearrange("(k p) c -> p k c", p=128)[q * 64:(q + 1) * 64],
                        eng="gpsimd")
            v_wo[j] = vv
        em.wait("sync", "v", bob_rel)
        v_bo = ld(bob2[:], bout_h[None, h * 1024:(h + 1) * 1024].broadcast_to((128, 1024)))
        if h == 1:
            em.wait("sync", "st", x2st[(0, 0)])
            x2tl_ld[0] = ld(x2tl[0][:], x2sp[0:128, 0:1024])
        for i in range(TILES):
            par = i % 2
            em.wait("sync", "v", xr_rel[par])
            v_xr = ld(xr2[par][:], x_h[i * 128:(i + 1) * 128, h * 1024:(h + 1) * 1024])
            ba, bb = (4, 5) if par == 0 else (2, 3)
            em.wait("tensor", "w", max(v_wo[0], v_wo[1]))
            em.wait("tensor", "v", p3_v)
            em.wait("tensor", "v", max(o_ps_rel[ba], o_ps_rel[bb]))
            v_omm = None
            for kc in range(KC):
                em.do("tensor", (lambda t, kc=kc, i=i, b=ba:
                                 t.matmul(ps[b][:], XB[:, kc * T + i * 128: kc * T + (i + 1) * 128],
                                          wo0_slice(kc),
                                          start=(kc == 0), stop=(kc == KC - 1))))
                fn = (lambda t, kc=kc, i=i, b=bb:
                      t.matmul(ps[b][:], XB[:, kc * T + i * 128: kc * T + (i + 1) * 128],
                               wob[1][:, kc * 512:(kc + 1) * 512],
                               start=(kc == 0), stop=(kc == KC - 1)))
                if kc == KC - 1:
                    v_omm = em.inc("tensor", fn, "mm", 1)
                else:
                    em.do("tensor", fn)
            if i == TILES - 1:
                wo_rel = v_omm
            em.wait("vector", "mm", v_omm)
            em.wait("vector", "ld", max(v_bo, v_xr))
            em.wait("vector", "st", x2h_rel[par])
            em.do("vector", lambda v, p=par, i=i:
                  v.scalar_tensor_tensor(x2h[p][:, 0:512], xr2[p][:, 0:512],
                                         c1t[:, i:i + 1], bob2[:, 0:512],
                                         ALU.mult, ALU.add))
            em.do("vector", lambda v, p=par, i=i:
                  v.scalar_tensor_tensor(x2h[p][:, 512:1024], xr2[p][:, 512:1024],
                                         c1t[:, i:i + 1], bob2[:, 512:1024],
                                         ALU.mult, ALU.add))
            em.dr("vector")
            v1 = em.inc("vector", lambda v, p=par, i=i, b=ba:
                        v.scalar_tensor_tensor(x2h[p][:, 0:512], ps[b][:],
                                               c2t[:, i:i + 1], x2h[p][:, 0:512],
                                               ALU.subtract, ALU.add), "v", 1)
            o_ps_rel[ba] = v1
            v2 = em.inc("vector", lambda v, p=par, i=i, b=bb:
                        v.scalar_tensor_tensor(x2h[p][:, 512:1024], ps[b][:],
                                               c2t[:, i:i + 1], x2h[p][:, 512:1024],
                                               ALU.subtract, ALU.add), "v", 1)
            o_ps_rel[bb] = v2
            xr_rel[par] = v2
            if i == TILES - 1:
                bob_rel = v2
            em.wait("sync", "v", v2)
            x2h_rel[par] = x2st[(h, i)] = st(
                x2sp[i * 128:(i + 1) * 128, h * 1024:(h + 1) * 1024], x2h[par][:])
            if h == 1:
                if i > 0:
                    p4b_tp(i - 1)
                p4b_ln(i)
                if i + 1 < TILES:
                    em.wait("sync", "v", x2tl_rel[(i + 1) % 2])
                    em.wait("sync", "st", x2st[(0, i + 1)])
                    x2tl_ld[i + 1] = ld(x2tl[(i + 1) % 2][:],
                                        x2sp[(i + 1) * 128:(i + 2) * 128, 0:1024])

    p4b_tp(TILES - 1)
    p4_pre_sc = em.val("sc")
    p4_pre_v = em.val("v")
    p4_pre_mm = em.val("mm")
    p4_mm = em.val("mm")
    p4_v = em.val("v")
    p4_sc = em.val("sc")
    if debug:
        em.wait("gpsimd", "sc", p4_sc)
        st(dbg["d_lnt2"], XA[:, 0:T], eng="gpsimd")
    for cm in reversed(p4cm):
        cm.__exit__(None, None, None)
    for cm in reversed(wob0cm):
        cm.__exit__(None, None, None)
    for cm in reversed(g4cm):
        cm.__exit__(None, None, None)

    p5cm = []
    wgub = [sb_scoped("wgub0", [128, KC * 256], BF16, p5cm),
            sb_scoped("wgub1", [128, KC * 256], BF16, p5cm)]
    wdb = [sb_scoped("wdb0", [128, NF * 128], BF16, p5cm),
           sb_scoped("wdb1", [128, NF * 128], BF16, p5cm)]
    XBE = sb_scoped("XBE", [128, NF * 1024 - KC * T], BF16, p5cm)
    gsc = sb_scoped("gsc", [128, 1024], BF16, p5cm)
    dsb = [sb_scoped("dsb0", [128, 1024], BF16, p5cm),
           sb_scoped("dsb1", [128, 1024], BF16, p5cm)]
    ysb = [sb_scoped(f"ysb{j}", [128, 128], F32, p5cm) for j in range(3)]
    x2c = [sb_scoped(f"x2c{j}", [128, 128], F32, p5cm) for j in range(3)]

    def hp_slice(f, lo, ln, fs):
        if f < 32:
            return XB[0:fs, f * 1024 + lo: f * 1024 + lo + ln]
        return XBE[0:fs, (f - 32) * 1024 + lo: (f - 32) * 1024 + lo + ln]
    bis = sb_scoped("bis", [128, 1024], F32, p5cm)
    bdump = sb_scoped("bdump", [128, 1024], F32, p5cm)
    smsc = sb_scoped("smsc", [128, 32], F32, p5cm)
    smtst = [sb_scoped("smtst0", [8, 128], F32, p5cm),
             sb_scoped("smtst1", [8, 128], F32, p5cm)]

    em.wait("gpsimd", "mm", p4_pre_mm)
    em.wait("gpsimd", "v", p4_pre_v)
    em.wait("gpsimd", "sc", p4_pre_sc)
    em.wait("sync", "v", p4_pre_v)

    gu_seq = [(gi, f) for gi in range(2) for f in range(NF)]
    v_wg = {}
    v_wu = {}
    wgu_rel = {}

    def issue_wgu(k):
        gi, f = gu_seq[k]
        em.wait("gpsimd", "mm", wgu_rel.get(k - 2, 0))
        v_wg[k] = v_wu[k] = ld(wgub[k % 2][:].rearrange("p (f c) -> p f c", c=256),
                               gu_h[f].rearrange("p (f c) -> p f c", c=256), eng="gpsimd")

    issue_wgu(0)
    issue_wgu(1)
    wd_seq = [(gi, dc) for gi in range(2) for dc in range(16)]
    v_wd = {}
    wd_rel = {}

    def issue_wd(k):
        gi, dc = wd_seq[k]
        em.wait("gpsimd", "mm", wd_rel.get(k - 2, 0))
        v_wd[k] = ld(wdb[k % 2][:].rearrange("p (f c) -> p f c", c=344),
                     wdi_h[dc].rearrange("p (f c) -> p f c", c=344), eng="gpsimd")


    cc_done = [False]
    v_cc = None
    v_bis = 0
    v_ccin = 0

    def emit_burst_and_cc():
        nonlocal v_cc, v_bis, v_ccin
        em.dr("vector")
        em.do("vector", lambda v: v.reduce_max(smsc[:, 0:16],
                                               smloc[:, :].rearrange("p (i e) -> p i e", e=E),
                                               axis=AX.X))
        em.dr("vector")
        for i in range(TILES):
            em.do("vector", lambda v, i=i: v.tensor_single_scalar(smloc[:, i * E:(i + 1) * E],
                                                                  smloc[:, i * E:(i + 1) * E],
                                                                  smsc[:, i:i + 1], ALU.subtract))
        em.dr("vector")
        vm_b = em.inc("vector", lambda v: v.engine_nop(), "v", 1)
        em.wait("scalar", "v", vm_b)
        se_b = em.inc("scalar", lambda s: s.activation(smloc[:, :], smloc[:, :], ACT.Exp), "sc", 1)
        em.wait("vector", "sc", se_b)
        em.do("vector", lambda v: v.reduce_sum(smsc[:, 0:16],
                                               smloc[:, :].rearrange("p (i e) -> p i e", e=E),
                                               axis=AX.X))
        em.dr("vector")
        em.do("vector", lambda v: v.reciprocal(smsc[:, 16:32], smsc[:, 0:16]))
        em.dr("vector")
        for i in range(TILES):
            em.do("vector", lambda v, i=i: v.tensor_single_scalar(smloc[:, i * E:(i + 1) * E],
                                                                  smloc[:, i * E:(i + 1) * E],
                                                                  smsc[:, 16 + i:17 + i], ALU.mult))
        em.dr("vector")
        v_smdone = em.inc("vector", lambda v: v.engine_nop(), "v", 1)
        v_ccst = 0
        smtst_rel = {0: 0, 1: 0}
        smt_bank_rel = {0: rt_v_rel[0], 1: 0}
        for i in range(TILES):
            bk = i % 2
            em.wait("tensor", "v", max(v_smdone, smt_bank_rel[bk]))
            v_tp2 = em.inc("tensor", lambda t, i=i, bk=bk:
                           t.transpose(ps[bk][0:8, 0:128], smloc[:, i * E:(i + 1) * E],
                                       identf[:]), "mm", 1)
            em.wait("vector", "mm", v_tp2)
            em.wait("vector", "st", smtst_rel[bk])
            smt_bank_rel[bk] = em.inc("vector", lambda v, i=i, bk=bk:
                                      v.tensor_copy(smtst[bk][:], ps[bk][0:8, 0:128]),
                                      "v", 1)
            em.wait("sync", "v", smt_bank_rel[bk])
            smtst_rel[bk] = v_ccst = st(cc_in.ap()[0:8, i * 128:(i + 1) * 128],
                                        smtst[bk][:])
        smt_rel[0] = max(smt_bank_rel[0], smt_bank_rel[1])
        em.wait("gpsimd", "st", v_ccst)
        v_cc = em.inc("gpsimd", lambda g: g.collective_compute(
            "AllGather", ALU.bypass, replica_groups=[list(range(B))],
            ins=[cc_in.ap()], outs=[cc_out.ap()]), "g", 1)
        em.wait("gpsimd", "g", v_cc)
        for e in range(E):
            for h in range(2):
                vv = ld(bis[e * 16 + h * 8: e * 16 + h * 8 + 8, :],
                        cc_out.ap()[e::8, h * 1024:(h + 1) * 1024], eng="gpsimd")
        globals()
        nonlocal_set(vv)

    def nonlocal_set(vv):
        nonlocal v_bis
        v_bis = vv

    _bis_masks = [[(l ^ j) for l in range(32)] for j in (1, 2, 4, 8)]

    def emit_bis_iter(first=False):
        if first:
            em.wait("vector", "w", v_bis)
            em.wait("vector", "g", v_cc)
        em.dr("vector")
        em.do("vector", lambda v: v.tensor_add(bst[:, 2:3], bst[:, 0:1], bst[:, 1:2]))
        em.dr("vector")
        em.do("vector", lambda v: v.tensor_scalar(bdump[:], bis[:], bst[:, 2:3], None,
                                                  ALU.is_ge, ALU.add,
                                                  accum_out=bst[:, 4:5]))
        for mk in _bis_masks:
            em.dr("vector")
            em.do("vector", lambda v, mk=mk: v.stream_shuffle(bst[:, 5:6], bst[:, 4:5], mk))
            em.dr("vector")
            em.do("vector", lambda v: v.tensor_add(bst[:, 4:5], bst[:, 4:5], bst[:, 5:6]))
        em.dr("vector")
        em.do("vector", lambda v: v.tensor_scalar(bst[:, 6:7], bst[:, 4:5],
                                                  float(KSEL) - 0.5, bst[:, 1:2],
                                                  ALU.is_ge, ALU.mult))
        em.dr("vector")
        em.do("vector", lambda v: v.tensor_add(bst[:, 0:1], bst[:, 0:1], bst[:, 6:7]))
        em.do("vector", lambda v: v.tensor_scalar_mul(bst[:, 1:2], bst[:, 1:2], 0.5))

    def finish_bisection():
        em.dr("vector")
        return em.inc("vector", lambda v: v.engine_nop(), "v", 1)

    def emit_tau_and_scol(v_bisdone):
        em.wait("gpsimd", "v", v_bisdone)
        for e in range(E):
            st(tau_d[0:1, e:e + 1], bst[e * 16:e * 16 + 1, 0:1], eng="gpsimd")
        em.wait("sync", "gst", em.val("gst"))
        v_taub = ld(taub[:], tau_d[0:1, :].broadcast_to((128, E)))
        em.wait("vector", "ld", v_taub)
        for i in range(TILES):
            em.do("vector", lambda v, i=i: v.tensor_tensor(r8b[:], smloc[:, i * E:(i + 1) * E],
                                                           taub[:], ALU.is_ge))
            em.dr("vector")
            em.do("vector", lambda v, i=i: v.tensor_mul(r8b[:], r8b[:],
                                                        smloc[:, i * E:(i + 1) * E]))
            em.dr("vector")
            em.do("vector", lambda v, i=i: v.reduce_sum(scol[:, i:i + 1], r8b[:], axis=AX.X))
            em.dr("vector")
        vs = em.inc("vector", lambda v: v.engine_nop(), "v", 1)
        if debug:
            em.wait("sync", "v", vs)
            st(dbg["d_scol"], scol[:])
            st(dbg["d_sm"], smloc[:])
            st(dbg["d_tau"], taub[:])
        return vs

    bis_left = BIS_ITERS
    v_scol = 0
    dtp_n = [0]
    tp_v_rel = {0: 0, 1: 0}
    dsb_mm_rel = {0: 0, 1: 0}
    dn_sb_rel = {0: 0, 1: 0}
    x2c_rel = {j: 0 for j in range(3)}
    ysb_rel = {j: 0 for j in range(3)}
    init01 = max(rt_v_rel[0], smt_rel[0])
    pair_v_rel = {(0, 1): init01, (2, 3): max(o_ps_rel[2], o_ps_rel[3]),
                  (4, 5): max(o_ps_rel[4], o_ps_rel[5])}
    pair_sc_rel = {}
    silu_rel = {}
    dn_ps_rel = dict(o_ps_rel)
    x2r_rel = {0: 0, 1: 0}
    ysc_rel = {0: 0, 1: 0}
    bdbt_rel = 0
    gu_idx = 0
    wd_idx = 0
    for gi, (t0, nt) in enumerate(FFN_GROUPS):
        Tg = nt * 128
        og = t0 * 128
        tcs = [(0, 512)] + ([(512, Tg - 512)] if Tg > 512 else [])
        for f in range(NF):
            k = gu_idx
            fo, fs = FCH[f]
            PAIRS = ((0, 1), (2, 3), (4, 5))
            gp = PAIRS[(k + 1) % 3]
            upp = PAIRS[(k + 2) % 3]
            em.wait("tensor", "w", max(v_wg[k], v_wu[k]))
            em.wait("tensor", "sc", p4_pre_sc)
            em.wait("tensor", "sc", pair_sc_rel.get(gp, 0))
            em.wait("tensor", "v", pair_v_rel.get(gp, 0))
            for kc in range(KC):
                for ti, (tco, tcl) in enumerate(tcs):
                    em.do("tensor", (lambda t, kc=kc, k=k, fs=fs, ti=ti, tco=tco, tcl=tcl,
                                     og=og, gp=gp:
                                     t.matmul(ps[gp[ti]][0:fs, 0:tcl],
                                              wgub[k % 2][:, kc * 256:kc * 256 + fs],
                                              XA[:, kc * T + og + tco: kc * T + og + tco + tcl],
                                              start=(kc == 0), stop=(kc == KC - 1))))
            em.wait("tensor", "sc", pair_sc_rel.get(upp, 0))
            em.wait("tensor", "v", pair_v_rel.get(upp, 0))
            v_gu = None
            for kc in range(KC):
                for ti, (tco, tcl) in enumerate(tcs):
                    fn = (lambda t, kc=kc, k=k, fs=fs, ti=ti, tco=tco, tcl=tcl, og=og, upp=upp:
                          t.matmul(ps[upp[ti]][0:fs, 0:tcl],
                                   wgub[k % 2][:, kc * 256 + 128:kc * 256 + 128 + fs],
                                   XA[:, kc * T + og + tco: kc * T + og + tco + tcl],
                                   start=(kc == 0), stop=(kc == KC - 1)))
                    if kc == KC - 1 and ti == len(tcs) - 1:
                        v_gu = em.inc("tensor", fn, "mm", 1)
                    else:
                        em.do("tensor", fn)
            wgu_rel[k] = v_gu
            em.wait("scalar", "mm", v_gu)
            for ti, (tco, tcl) in enumerate(tcs):
                fn = (lambda s, ti=ti, tco=tco, tcl=tcl, f=f, fs=fs, gp=gp:
                      s.activation(gsc[0:fs, tco:tco + tcl], ps[gp[ti]][0:fs, 0:tcl],
                                   ACT.Silu, bias=bgpp[0:fs, f:f + 1]))
                if ti == len(tcs) - 1:
                    pair_sc_rel[gp] = em.inc("scalar", fn, "sc", 1)
                else:
                    em.do("scalar", fn)
            em.wait("vector", "mm", v_gu)
            em.wait("vector", "sc", pair_sc_rel[gp])
            for ti, (tco, tcl) in enumerate(tcs):
                fn = (lambda v, ti=ti, tco=tco, tcl=tcl, f=f, fs=fs, upp=upp:
                      v.scalar_tensor_tensor(hp_slice(f, tco, tcl, fs),
                                             ps[upp[ti]][0:fs, 0:tcl], bupp[0:fs, f:f + 1],
                                             gsc[0:fs, tco:tco + tcl], ALU.add, ALU.mult))
                if ti == len(tcs) - 1:
                    pair_v_rel[upp] = em.inc("vector", fn, "v", 1)
                else:
                    em.do("vector", fn)
            gu_idx += 1
            if gu_idx + 1 < len(gu_seq):
                issue_wgu(gu_idx + 1)
            if gi == 0 and f == 3 and not cc_done[0]:
                emit_burst_and_cc()
                pair_v_rel[(0, 1)] = max(pair_v_rel.get((0, 1), 0), smt_rel[0])
                cc_done[0] = True
            if gi == 0 and f >= 6 and bis_left > 0:
                n_it = min(2, bis_left)
                for _ in range(n_it):
                    emit_bis_iter(first=(bis_left == BIS_ITERS))
                    bis_left -= 1
                if bis_left == 0:
                    v_bisdone = finish_bisection()
            if gi == 0 and f == 29 and not v_scol:
                v_scol = emit_tau_and_scol(v_bisdone)
            if gi == 0 and f == 36:
                issue_wd(0)
            if gi == 0 and f == 39:
                issue_wd(1)
        hp_done = max(pair_v_rel[(0, 1)] if (0, 1) in pair_v_rel else 0,
                      pair_v_rel.get((2, 3), 0), pair_v_rel.get((4, 5), 0))
        PAIRS = ((0, 1), (2, 3), (4, 5))

        def dn_evac_unit(dc, tc):
            gidx = t0 + tc
            dpar = dc % 2
            slot = dtp_n[0] % 2
            xs = dtp_n[0] % 3
            em.wait("tensor", "sc", dn_sb_rel[dpar])
            em.wait("tensor", "v", tp_v_rel[slot])
            v_tp = em.inc("tensor", lambda t, dp=dpar, tc=tc, sl=slot:
                          t.transpose(psTp[sl][:, 0:128],
                                      dsb[dp][:, tc * 128:(tc + 1) * 128],
                                      identb[:]), "mm", 1)
            if tc == 7:
                dsb_mm_rel[dpar] = v_tp
            em.wait("sync", "v", x2c_rel[xs])
            v_x2 = ld(x2c[xs][:],
                      x2sp[og + tc * 128: og + (tc + 1) * 128,
                           dc * 128:(dc + 1) * 128])
            em.wait("vector", "mm", v_tp)
            em.wait("vector", "ld", v_x2)
            em.wait("vector", "v", v_scol)
            em.wait("vector", "st", ysb_rel[xs])
            v_y = em.inc("vector", lambda v, sl=slot, xs=xs, gidx=gidx:
                         v.scalar_tensor_tensor(ysb[xs][:], psTp[sl][:, 0:128],
                                                scol[:, gidx:gidx + 1], x2c[xs][:],
                                                ALU.mult, ALU.add), "v", 1)
            tp_v_rel[slot] = v_y
            x2c_rel[xs] = v_y
            em.wait("sync", "v", v_y)
            ysb_rel[xs] = st(y_h[og + tc * 128: og + (tc + 1) * 128,
                                 dc * 128:(dc + 1) * 128], ysb[xs][:])
            dtp_n[0] += 1

        EVAC_AT = (3, 8, 13, 18, 23, 28, 33, 38)
        for dc in range(16):
            k = wd_idx
            pr = PAIRS[dc % 3]
            dpar = dc % 2
            em.wait("tensor", "w", v_wd[k])
            em.wait("tensor", "v", hp_done)
            em.wait("tensor", "v", pair_v_rel.get(pr, 0))
            em.wait("tensor", "sc", pair_sc_rel.get(pr, 0))
            v_dmm = None
            for f in range(NF):
                fo, fs = FCH[f]
                for half in range(2):
                    fn = (lambda t, f=f, fs=fs, k=k, b=pr[half], half=half:
                          t.matmul(ps[b][:, 0:512],
                                   wdb[k % 2][0:fs, f * 128:(f + 1) * 128],
                                   hp_slice(f, half * 512, 512, fs),
                                   start=(f == 0), stop=(f == NF - 1)))
                    if f == NF - 1 and half == 1:
                        v_dmm = em.inc("tensor", fn, "mm", 1)
                    else:
                        em.do("tensor", fn)
                if dc > 0 and f in EVAC_AT:
                    dn_evac_unit(dc - 1, EVAC_AT.index(f))
            wd_rel[k] = v_dmm
            em.wait("scalar", "mm", v_dmm)
            em.wait("scalar", "mm", dsb_mm_rel[dpar])
            em.do("scalar", lambda s, b=pr[0], dp=dpar, dc=dc:
                  s.activation(dsb[dp][:, 0:512], ps[b][:, 0:512], ACT.Identity,
                               bias=bdcols[:, dc:dc + 1]))
            sB = em.inc("scalar", lambda s, b=pr[1], dp=dpar, dc=dc:
                        s.activation(dsb[dp][:, 512:1024], ps[b][:, 0:512], ACT.Identity,
                                     bias=bdcols[:, dc:dc + 1]), "sc", 1)
            dn_sb_rel[dpar] = sB
            pair_sc_rel[pr] = sB
            if k + 2 < len(wd_seq):
                issue_wd(k + 2)
            wd_idx += 1
        for tc in range(8):
            dn_evac_unit(15, tc)

    if debug:
        st(dbg["d_x2"], x2sp)
        em.wait("gpsimd", "sc", p4_sc)
        st(dbg["d_lnt"], XA[:, 0:T], eng="gpsimd")
    em.wait("gpsimd", "st", em.val("st"))
    em.wait("gpsimd", "gst", em.val("gst"))

    with nc.Block() as block:
        @block.sync
        def _(e):
            for fn in em.ops["sync"]:
                fn(e)

        @block.gpsimd
        def _(e):
            for fn in em.ops["gpsimd"]:
                fn(e)

        @block.tensor
        def _(e):
            for fn in em.ops["tensor"]:
                fn(e)

        @block.vector
        def _(e):
            for fn in em.ops["vector"]:
                fn(e)

        @block.scalar
        def _(e):
            for fn in em.ops["scalar"]:
                fn(e)

    for cm in reversed(p5cm):
        cm.__exit__(None, None, None)
    for cm in reversed(ctxs):
        cm.__exit__(None, None, None)
    return nc, dbg


_NC_CACHE = {}


def kernel(L0_attn_X, L0_attn_W_in, L0_attn_b_in, L0_attn_W_out, L0_attn_b_out,
           L0_attn_gamma, L0_attn_beta, L0_ffn_router_w, L0_ffn_router_b,
           L0_ffn_w_gate, L0_ffn_b_gate, L0_ffn_w_up, L0_ffn_b_up,
           L0_ffn_w_down, L0_ffn_b_down, _debug=False):
    key = ("prog", _debug)
    if key not in _NC_CACHE:
        _NC_CACHE[key] = build_program(debug=_debug)
    nc, dbg = _NC_CACHE[key]

    cos_t, sin_t = _rotary_tables()
    gum = _gumbel_table()
    wg32 = np.asarray(L0_ffn_w_gate, np.float32)
    wu32 = np.asarray(L0_ffn_w_up, np.float32)
    wd32 = np.asarray(L0_ffn_w_down, np.float32)
    gu_img = np.zeros((D, NF * 256), np.float32)
    wd_img = np.zeros((NF * 128, D), np.float32)
    KC_, NF_ = D // 128, NF
    for f, (fo, fs) in enumerate(FCH):
        gu_img[:, f * 256:f * 256 + fs] = wg32[:, fo:fo + fs]
        gu_img[:, f * 256 + 128:f * 256 + 128 + fs] = wu32[:, fo:fo + fs]
        wd_img[f * 128:f * 128 + fs, :] = wd32[fo:fo + fs, :]
    common = dict(
        win=np.ascontiguousarray(np.asarray(L0_attn_W_in, np.float32)),
        bin=np.ascontiguousarray(np.asarray(L0_attn_b_in, np.float32)),
        wout=np.ascontiguousarray(np.asarray(L0_attn_W_out, np.float32)),
        bout=np.ascontiguousarray(np.asarray(L0_attn_b_out, np.float32)),
        gamma=np.ascontiguousarray(np.asarray(L0_attn_gamma, np.float32)),
        beta=np.ascontiguousarray(np.asarray(L0_attn_beta, np.float32)),
        rw=np.ascontiguousarray(np.asarray(L0_ffn_router_w, np.float32)),
        rb=np.ascontiguousarray(np.asarray(L0_ffn_router_b, np.float32)),
        gu_img=gu_img.reshape(KC_, 128, NF_, 256).transpose(2, 1, 0, 3)
               .reshape(NF_, 128, KC_ * 256).copy(),
        bg=np.ascontiguousarray(np.asarray(L0_ffn_b_gate, np.float32)),
        bu=np.ascontiguousarray(np.asarray(L0_ffn_b_up, np.float32)),
        wd_img=wd_img.reshape(NF_, 128, 16, 128).transpose(2, 1, 0, 3)
               .reshape(16, 128, NF_ * 128).copy(),
        bd=np.ascontiguousarray(np.asarray(L0_ffn_b_down, np.float32)),
        cost=cos_t, sint=sin_t,
        identb=np.eye(128, dtype=BF), identf=np.eye(128, dtype=np.float32),
        cvec=np.array([0.0, 0.5, EPS, 1.0], dtype=np.float32),
    )
    X = np.asarray(L0_attn_X, dtype=np.float32)
    in_maps = []
    for b in range(B):
        m = dict(common)
        m["x"] = np.ascontiguousarray(X[b])
        m["gumb"] = np.ascontiguousarray(gum[b * T:(b + 1) * T])
        in_maps.append(m)

    res = run_bass_kernel_spmd(nc, in_maps, core_ids=list(range(B)))
    out = np.stack([r["y"] for r in res.results], axis=0).astype(np.float32)
    if _debug:
        return out, res
    return out


# version 19
# speedup vs baseline: 1.0108x; 1.0108x over previous
import numpy as np
import ml_dtypes
import concourse.bass as bass
import concourse.mybir as mybir
from concourse.bass_utils import run_bass_kernel_spmd

F32 = mybir.dt.float32
BF16 = mybir.dt.bfloat16
AX = mybir.AxisListType
ALU = mybir.AluOpType
ACT = mybir.ActivationFunctionType
BF = ml_dtypes.bfloat16

B, T, D, DFF, E = 8, 2048, 2048, 5461, 8
TILES = T // 128
KC = D // 128
ROT = 1024
NTOK = B * T
KSEL = NTOK // E
FCH = [(f * 128, min(128, DFF - f * 128)) for f in range((DFF + 127) // 128)]
NF = len(FCH)
FFN_GROUPS = [(0, 8), (8, 8)]
BIS_ITERS = 28
EPS = 1e-5


def _gumbel_table():
    def threefry(k0, k1, x0, x1):
        rot = [(13, 15, 26, 6), (17, 29, 16, 24)]
        ks = [np.uint32(k0), np.uint32(k1),
              np.uint32(k0) ^ np.uint32(k1) ^ np.uint32(0x1BD11BDA)]

        def rotl(x, r):
            r = np.uint32(r)
            return ((x << r) | (x >> (np.uint32(32) - r))).astype(np.uint32)

        with np.errstate(over="ignore"):
            x0 = (x0.astype(np.uint32) + ks[0]).astype(np.uint32)
            x1 = (x1.astype(np.uint32) + ks[1]).astype(np.uint32)
            for i in range(5):
                for r in rot[i % 2]:
                    x0 = (x0 + x1).astype(np.uint32)
                    x1 = (rotl(x1, r) ^ x0).astype(np.uint32)
                x0 = (x0 + ks[(i + 1) % 3]).astype(np.uint32)
                x1 = (x1 + ks[(i + 2) % 3] + np.uint32(i + 1)).astype(np.uint32)
        return x0, x1

    n = NTOK * E
    iota = np.arange(n, dtype=np.uint64)
    b1, b2 = threefry(0, 42, (iota >> np.uint64(32)).astype(np.uint32),
                      (iota & np.uint64(0xFFFFFFFF)).astype(np.uint32))
    bits = (b1 ^ b2).astype(np.uint32)
    u = ((bits >> np.uint32(9)) | np.uint32(0x3F800000)).view(np.float32) - np.float32(1.0)
    eps = np.float32(1e-10)
    g = -np.log((-np.log(u + eps, dtype=np.float32)) + eps, dtype=np.float32)
    return g.reshape(NTOK, E).astype(np.float32)


def _rotary_tables():
    inv_freq = 1.0 / (10000.0 ** (np.arange(0, ROT, 2, dtype=np.float64) / ROT))
    freqs = np.arange(T, dtype=np.float64)[:, None] * inv_freq[None, :]
    freqs = np.repeat(freqs, 2, axis=-1)
    return np.cos(freqs).astype(np.float32), np.sin(freqs).astype(np.float32)


class P:

    ENGS = ("sync", "gpsimd", "tensor", "vector", "scalar")

    def __init__(self):
        self.ops = {k: [] for k in self.ENGS}
        self.cnt = {}
        self.sems = {}
        self.lastwait = {k: {} for k in self.ENGS}

    def wait(self, eng, sname, val):
        if val is None or val <= 0:
            return
        if self.lastwait[eng].get(sname, -1) >= val:
            return
        self.lastwait[eng][sname] = val
        h = self.sems[sname]
        self.ops[eng].append(lambda e, h=h, v=val: e.wait_ge(h, v))

    def do(self, eng, fn):
        self.ops[eng].append(fn)

    def dr(self, eng):
        self.ops[eng].append(lambda e: e.drain())

    def inc(self, eng, fn, sname, by):
        v = self.cnt.get(sname, 0) + by
        self.cnt[sname] = v
        h = self.sems[sname]
        self.ops[eng].append(lambda e, fn=fn, h=h, by=by: fn(e).then_inc(h, by))
        return v

    def val(self, sname):
        return self.cnt.get(sname, 0)


def build_program(debug=False):
    nc = bass.Bass("TRN2", target_bir_lowering=False, detect_race_conditions=False)

    def din(name, shape, dt=F32):
        return nc.dram_tensor(name, list(shape), dt, kind="ExternalInput").ap()

    x_h = din("x", (T, D))
    win_h = din("win", (D, 3 * D))
    bin_h = din("bin", (3 * D,))
    wout_h = din("wout", (D, D))
    bout_h = din("bout", (D,))
    gam_h = din("gamma", (D,))
    bet_h = din("beta", (D,))
    rw_h = din("rw", (D, E))
    rb_h = din("rb", (E,))
    gu_h = din("gu_img", (NF, 128, KC * 256))
    bg_h = din("bg", (DFF,))
    bu_h = din("bu", (DFF,))
    wdi_h = din("wd_img", (16, 128, NF * 128))
    bd_h = din("bd", (D,))
    cos_h = din("cost", (T, ROT))
    sin_h = din("sint", (T, ROT))
    gum_h = din("gumb", (T, E))
    idb_h = din("identb", (128, 128), BF16)
    idf_h = din("identf", (128, 128), F32)
    cst_h = din("cvec", (4,))

    y_h = nc.dram_tensor("y", [T, D], F32, kind="ExternalOutput").ap()
    x2sp = nc.dram_tensor("x2spill", [T, D], F32, kind="Internal").ap()
    tau_d = nc.dram_tensor("tau_d", [1, E], F32, kind="Internal").ap()
    cc_in = nc.dram_tensor("cc_in", [E, T], F32, kind="Internal")
    cc_out = nc.dram_tensor("cc_out", [B * E, T], F32, kind="Internal")

    dbg = {}
    if debug:
        for nm, shape in [("d_at", (128, KC)), ("d_sm", (128, TILES * E)),
                          ("d_tau", (128, E)), ("d_scol", (128, TILES)),
                          ("d_invn", (128, TILES)), ("d_x2", (T, D)),
                          ("d_lnt", (128, T)), ("d_lnt2", (128, T)),
                          ("d_x2t", (128, D)), ("d_xnb0", (128, D))]:
            dbg[nm] = nc.dram_tensor(nm, list(shape), F32, kind="ExternalOutput").ap()

    em = P()
    ctxs = []

    def sb(name, shape, dt):
        cm = nc.sbuf_tensor(name, list(shape), dt)
        h = cm.__enter__()
        ctxs.append(cm)
        return h

    def sb_scoped(name, shape, dt, bag):
        cm = nc.sbuf_tensor(name, list(shape), dt)
        h = cm.__enter__()
        bag.append(cm)
        return h

    def psum(name, shape, dt=F32):
        cm = nc.psum_tensor(name, list(shape), dt)
        h = cm.__enter__()
        ctxs.append(cm)
        return h

    for sname in ("ld", "st", "w", "gst", "g", "mm", "v", "sc"):
        cm = nc.semaphore("s_" + sname)
        em.sems[sname] = cm.__enter__()
        ctxs.append(cm)

    XA = sb("XA", [128, KC * T], BF16)
    XBW = KC * T
    XB = sb("XB", [128, XBW], BF16)
    smloc = sb("smloc", [128, TILES * E], F32)
    scol = sb("scol", [128, TILES], F32)
    taub = sb("taub", [128, E], F32)
    identb = sb("identbs", [128, 128], BF16)
    identf = sb("identfs", [128, 128], F32)
    bgpp = sb("bgpp", [128, NF], F32)
    bupp = sb("bupp", [128, NF], F32)
    bdcols = sb("bdcols", [128, 16], F32)
    r8b = sb("r8b", [128, E], F32)
    bst = sb("bst", [128, 8], F32)

    ps = [psum(f"ps{i}", [128, 512]) for i in range(6)]
    psTp = [psum("psTa", [128, 1024], BF16), psum("psTb", [128, 1024], BF16)]

    def ld(dst, src, eng="sync"):
        sname = "ld" if eng == "sync" else "w"
        return em.inc(eng, lambda e, d=dst, s=src: e.dma_start(out=d, in_=s), sname, 16)

    def st(dst, src, eng="sync"):
        sname = "st" if eng == "sync" else "gst"
        return em.inc(eng, lambda e, d=dst, s=src: e.dma_start(out=d, in_=s), sname, 16)

    cst3 = sb("cst3", [128, 4], F32)
    g4cm = []
    stg = sb_scoped("stg", [128, 128], F32, g4cm)
    gum_sb = sb_scoped("gum_sb", [128, TILES * E], F32, g4cm)
    rwb = sb_scoped("rwb", [128, KC * E], BF16, g4cm)
    rbb = sb_scoped("rbb", [128, E], F32, g4cm)
    r8a = sb_scoped("r8a", [128, E], F32, g4cm)
    mu1 = sb_scoped("mu1", [128, TILES], F32, g4cm)
    rstd1 = sb_scoped("rstd1", [128, TILES], F32, g4cm)
    c1t = sb_scoped("c1t", [128, TILES], F32, g4cm)
    c2t = sb_scoped("c2t", [128, TILES], F32, g4cm)
    nk2 = sb_scoped("nk2", [128, TILES * 4], F32, g4cm)
    nv2 = sb_scoped("nv2", [128, TILES * 4], F32, g4cm)
    nk2r = sb_scoped("nk2r", [128, TILES], F32, g4cm)
    nv2r = sb_scoped("nv2r", [128, TILES], F32, g4cm)
    invnb = sb_scoped("invnb", [128, TILES], BF16, g4cm)
    invnf = sb_scoped("invnf", [128, TILES], F32, g4cm)
    at_s = sb_scoped("at_s", [128, KC], F32, g4cm)
    binq = sb_scoped("binq", [128, KC], F32, g4cm)
    csc = sb_scoped("csc", [128, 16], F32, g4cm)
    betc = sb_scoped("betc", [128, KC], F32, g4cm)
    nc.const_aps.register(F32, 0.0, cst3[:, 0:1])
    nc.const_aps.register(F32, 0.5, cst3[:, 1:2])
    nc.const_aps.register(F32, EPS, cst3[:, 2:3])
    nc.const_aps.register(F32, 1.0, cst3[:, 3:4])
    ld(cst3[:], cst_h[None, :].broadcast_to((128, 4)))
    ld(rbb[:], rb_h[None, :].broadcast_to((128, E)))
    ld(identb[:], idb_h)
    ld(identf[:], idf_h)
    ld(gum_sb[:].rearrange("p (i e) -> p i e", e=E), gum_h.rearrange("(i p) e -> p i e", p=128))
    ld(bst[:, 0:2], cst_h[None, 0:2].broadcast_to((128, 2)))
    ld(bgpp[0:85, 42:43], bg_h[5376:DFF, None])
    ld(bupp[0:85, 42:43], bu_h[5376:DFF, None])
    v_rwb = ld(rwb[:].rearrange("p (k e) -> p k e", e=E),
               rw_h.rearrange("(k p) e -> p k e", p=128), eng="gpsimd")
    stp_rel = 0
    for dst, src_ap, rows in ((binq, bin_h[0:D], 16), (bgpp, bg_h[0:5376], 42),
                              (bupp, bu_h[0:5376], 42), (bdcols, bd_h[0:D], 16),
                              (betc, bet_h[0:D], 16)):
        em.wait("sync", "sc", stp_rel)
        v_stg = ld(stg[0:rows, :], src_ap.rearrange("(q p) -> q p", p=128))
        em.wait("tensor", "ld", v_stg)
        v_tp0 = em.inc("tensor", lambda t, rows=rows:
                       t.transpose(ps[0][:, 0:rows], stg[0:rows, :], identf[0:rows, 0:rows]), "mm", 1)
        em.wait("scalar", "mm", v_tp0)
        stp_rel = em.inc("scalar", lambda s, dst=dst, rows=rows:
                         s.copy(dst[:, 0:rows], ps[0][:, 0:rows]), "sc", 1)
    setup_sc = em.val("sc")

    WO_SPL = 10
    wob0cm = []
    wob0 = sb_scoped("wob0", [128, WO_SPL * 512], BF16, wob0cm)
    wkv0cm = []
    wkv0 = sb_scoped("wkv0", [128, KC * 512], BF16, wkv0cm)

    p1cm = []
    xb3 = [sb_scoped(f"xb3_{j}", [128, D], F32, p1cm) for j in range(3)]
    gamt = sb_scoped("gamt", [128, D], BF16, p1cm)
    sqdump = sb_scoped("sqdump", [128, D], F32, p1cm)
    xnb = [sb_scoped("xnb0", [128, D], BF16, p1cm),
           sb_scoped("xnb1", [128, D], BF16, p1cm)]
    csc2 = sb_scoped("csc2", [128, 36], F32, p1cm)

    ld(gamt[:], gam_h[None, :].broadcast_to((128, D)), eng="gpsimd")
    v_gb = v_wk0 = ld(wkv0[:].rearrange("p (k c) -> p k c", c=512),
                      win_h[:, D:D + 512].rearrange("(k p) c -> p k c", p=128),
                      eng="gpsimd")

    tp_evac = {0: 0, 1: 0}
    xnb_rel = {}
    xb_rel = {0: 0, 1: 0, 2: 0}
    v_x = {}
    s0v = {}
    s1s = {}
    s2v = {}
    s3s = {}
    v_xnd = {}
    tpn = 0

    def cc(t, k):
        g = (t % 3) * 12
        return csc2[:, g + k:g + k + 1]

    def s0(t):
        em.wait("vector", "ld", v_x[t])
        em.wait("vector", "w", v_gb)
        em.do("vector", lambda v, t=t: v.reduce_sum(cc(t, 0), xb3[t % 3][:], axis=AX.X))
        em.dr("vector")
        em.do("vector", lambda v, t=t: v.tensor_scalar_mul(mu1[:, t:t + 1], cc(t, 0), 1.0 / D))
        s0v[t] = em.inc("vector", lambda v, t=t:
                        v.tensor_scalar_mul(cc(t, 2), cc(t, 0), -1.0 / D), "v", 1)

    def s1(t):
        em.wait("scalar", "v", s0v[t])
        em.do("scalar", lambda s, t=t: s.activation(sqdump[:], xb3[t % 3][:], ACT.Square,
                                                    bias=cc(t, 2), accum_out=cc(t, 3)))
        em.dr("scalar")
        s1s[t] = em.inc("scalar", lambda s, t=t: s.activation(cc(t, 4), cc(t, 3), ACT.Sqrt,
                                                              bias=EPS, scale=1.0 / D), "sc", 1)

    def s2(t):
        em.wait("vector", "sc", s1s[t])
        em.do("vector", lambda v, t=t: v.reciprocal(rstd1[:, t:t + 1], cc(t, 4)))
        em.dr("vector")
        em.do("vector", lambda v, t=t: v.tensor_scalar_add(c1t[:, t:t + 1], rstd1[:, t:t + 1], 1.0))
        em.do("vector", lambda v, t=t: v.tensor_mul(c2t[:, t:t + 1], mu1[:, t:t + 1],
                                                    rstd1[:, t:t + 1]))
        em.do("vector", lambda v, t=t: v.tensor_mul(cc(t, 5), rstd1[:, t:t + 1],
                                                    rstd1[:, t:t + 1]))
        em.dr("vector")
        s2v[t] = em.inc("vector", lambda v, t=t: v.tensor_mul(cc(t, 6), cc(t, 3), cc(t, 5)),
                        "v", 1)

    def s3(t):
        em.wait("scalar", "v", s2v[t])
        s3s[t] = em.inc("scalar", lambda s, t=t: s.activation(cc(t, 9), cc(t, 6), ACT.Sqrt,
                                                              bias=EPS, scale=1.0 / D), "sc", 1)

    def s4(t):
        par = t % 2
        em.wait("vector", "sc", s3s[t])
        em.wait("vector", "mm", xnb_rel.get(par, 0))
        em.do("vector", lambda v, t=t: v.reciprocal(cc(t, 10), cc(t, 9)))
        em.dr("vector")
        em.do("vector", lambda v, t=t: v.tensor_mul(cc(t, 11), cc(t, 10), rstd1[:, t:t + 1]))
        em.dr("vector")
        em.do("vector", lambda v, t=t: v.tensor_scalar(xb3[t % 3][:], xb3[t % 3][:],
                                                       mu1[:, t:t + 1], cc(t, 11),
                                                       ALU.subtract, ALU.mult))
        em.dr("vector")
        v_xn = em.inc("vector", lambda v, t=t, p=par: v.tensor_mul(xnb[p][:], xb3[t % 3][:],
                                                                   gamt[:]), "v", 1)
        xb_rel[t % 3] = v_xn
        v_xnd[t] = v_xn

    def s5(t):
        nonlocal_tpn = [0]
        par = t % 2
        em.wait("tensor", "v", v_xnd[t])
        for kc in range(KC):
            slot = (t * KC + kc) % 2
            em.wait("tensor", "sc", tp_evac[slot])
            v_tp = em.inc("tensor", lambda tt, p=par, kc=kc, sl=slot:
                          tt.transpose(psTp[sl][:, 0:128], xnb[p][:, kc * 128:(kc + 1) * 128],
                                       identb[:]), "mm", 1)
            if kc == KC - 1:
                xnb_rel[par] = v_tp
            em.wait("scalar", "mm", v_tp)
            tp_evac[slot] = em.inc("scalar", lambda s, t=t, kc=kc, sl=slot:
                                   s.activation(XA[:, kc * T + t * 128: kc * T + (t + 1) * 128],
                                                psTp[sl][:, 0:128], ACT.Identity,
                                                bias=betc[:, kc:kc + 1]), "sc", 1)

    for t in range(TILES + 2):
        if t < TILES:
            em.wait("sync", "v", xb_rel[t % 3])
            v_x[t] = ld(xb3[t % 3][:], x_h[t * 128:(t + 1) * 128, :])
            s0(t)
            s1(t)
        if 1 <= t <= TILES and t - 1 < TILES:
            s2(t - 1)
            s3(t - 1)
        if t >= 2:
            s4(t - 2)
            s5(t - 2)

    p1_mm = em.val("mm")
    p1_sc = em.val("sc")
    p1_v = em.val("v")
    for cm in reversed(p1cm):
        cm.__exit__(None, None, None)

    p2cm = []
    wkv = [wkv0, sb_scoped("wkv1", [128, KC * 512], BF16, p2cm)]
    cstl = [sb_scoped("cst0", [128, 512], F32, p2cm),
            sb_scoped("cst1", [128, 512], F32, p2cm)]
    sntl = [sb_scoped("snt0", [128, 512], F32, p2cm),
            sb_scoped("snt1", [128, 512], F32, p2cm)]
    bkt = [sb_scoped("bkt0", [128, 512], F32, p2cm),
           sb_scoped("bkt1", [128, 512], F32, p2cm)]
    kbuf2 = [sb_scoped("kbuf0", [128, 512], F32, p2cm),
             sb_scoped("kbuf1", [128, 512], F32, p2cm)]
    rs1 = sb_scoped("rs1", [128, 256], F32, p2cm)
    rs2 = sb_scoped("rs2", [128, 256], F32, p2cm)
    vrb = sb_scoped("vrb", [128, 512], F32, p2cm)
    sq5 = sb_scoped("sq5", [128, 512], F32, p2cm)

    em.wait("gpsimd", "mm", p1_mm)
    em.wait("gpsimd", "sc", p1_sc)
    em.wait("gpsimd", "v", p1_v)
    em.wait("sync", "sc", p1_sc)
    em.wait("sync", "v", p1_v)

    wseq = [("K", n) for n in range(4)] + [("V", n) for n in range(4)]
    v_wkv = {}
    wkv_rel = {}

    def issue_wkv(idx):
        kind, n = wseq[idx]
        colbase = D + (0 if kind == "K" else D) + n * 512
        em.wait("gpsimd", "mm", wkv_rel.get(idx - 2, 0))
        v_wkv[idx] = ld(wkv[idx % 2][:].rearrange("p (k c) -> p k c", c=512),
                        win_h[:, colbase:colbase + 512].rearrange("(k p) c -> p k c", p=128),
                        eng="gpsimd")

    v_wkv[0] = v_wk0
    issue_wkv(1)
    v_wo0_pref = ld(wob0[:].rearrange("p (k c) -> p k c", c=512),
                    wout_h[0:WO_SPL * 128, 0:512].rearrange("(k p) c -> p k c", p=128), eng="gpsimd")

    psum_rel = {4: 0, 5: 0}
    sq_rel2 = {}
    cs_rel = {0: 0, 1: 0}
    bkt_rel = {0: 0, 1: 0}

    for idx, (kind, n) in enumerate(wseq):
        colbase = D + (0 if kind == "K" else D) + n * 512
        em.wait("sync", "v", bkt_rel[n % 2])
        v_bk = ld(bkt[n % 2][:], bin_h[None, colbase:colbase + 512].broadcast_to((128, 512)))
        for i in range(TILES):
            v_cs = 0
            if n < 2:
                em.wait("sync", "v", cs_rel[i % 2])
                ld(cstl[i % 2][:], cos_h[i * 128:(i + 1) * 128, n * 512:(n + 1) * 512])
                v_cs = ld(sntl[i % 2][:], sin_h[i * 128:(i + 1) * 128, n * 512:(n + 1) * 512])
            bank = 4 + (idx * TILES + i) % 2
            em.wait("tensor", "w", v_wkv[idx])
            em.wait("tensor", "sc", p1_sc)
            em.wait("tensor", "v", psum_rel[bank])
            v_mm = None
            for kc in range(KC):
                fn = (lambda t, kc=kc, i=i, b=bank, p=idx % 2:
                      t.matmul(ps[b][:],
                               XA[:, kc * T + i * 128: kc * T + (i + 1) * 128],
                               wkv[p][:, kc * 512:(kc + 1) * 512],
                               start=(kc == 0), stop=(kc == KC - 1)))
                if kc == KC - 1:
                    v_mm = em.inc("tensor", fn, "mm", 1)
                else:
                    em.do("tensor", fn)
            if i == TILES - 1:
                wkv_rel[idx] = v_mm
            kpar = (idx * TILES + i) % 2
            kbuf = kbuf2[kpar]
            em.wait("vector", "mm", v_mm)
            em.wait("vector", "ld", v_bk)
            em.wait("vector", "sc", sq_rel2.get(kpar, 0))
            v_ev = em.inc("vector", lambda v, b=bank, p=n % 2, kb=kbuf:
                          v.tensor_add(kb[:], ps[b][:], bkt[p][:]), "v", 1)
            psum_rel[bank] = v_ev
            if i == TILES - 1:
                bkt_rel[n % 2] = v_ev
            nrm = nk2 if kind == "K" else nv2
            em.wait("scalar", "v", v_ev)
            sq_rel2[kpar] = em.inc("scalar", lambda s, nrm=nrm, i=i, n=n, kb=kbuf:
                                   s.activation(sq5[:], kb[:], ACT.Square,
                                                accum_out=nrm[:, i * 4 + n: i * 4 + n + 1]), "sc", 1)
            base = i * D + n * 512
            if n < 2:
                em.wait("vector", "ld", v_cs)
                ke, ko = kbuf[:, 0:512:2], kbuf[:, 1:512:2]
                ce, co = cstl[i % 2][:, 0:512:2], cstl[i % 2][:, 1:512:2]
                se, so = sntl[i % 2][:, 0:512:2], sntl[i % 2][:, 1:512:2]
                if kind == "K":
                    de = XB[:, base:base + 512:2]
                    do = XB[:, base + 1:base + 512:2]
                else:
                    de = vrb[:, 0:512:2]
                    do = vrb[:, 1:512:2]
                em.dr("vector")
                em.do("vector", lambda v, ke=ke, ce=ce: v.tensor_mul(rs1[:], ke, ce))
                em.do("vector", lambda v, ko=ko, se=se: v.tensor_mul(rs2[:], ko, se))
                em.dr("vector")
                em.do("vector", lambda v, de=de: v.tensor_sub(de, rs1[:], rs2[:]))
                em.do("vector", lambda v, ko=ko, co=co: v.tensor_mul(rs1[:], ko, co))
                em.do("vector", lambda v, ke=ke, so=so: v.tensor_mul(rs2[:], ke, so))
                em.dr("vector")
                vr = em.inc("vector", lambda v, do=do: v.tensor_add(do, rs1[:], rs2[:]), "v", 1)
                cs_rel[i % 2] = vr
                if kind == "V":
                    em.dr("vector")
                    em.inc("vector", lambda v, base=base:
                           v.tensor_mul(XB[:, base:base + 512], XB[:, base:base + 512],
                                        vrb[:]), "v", 1)
            else:
                em.dr("vector")
                if kind == "K":
                    em.inc("vector", lambda v, base=base:
                           v.tensor_copy(XB[:, base:base + 512], kbuf[:]), "v", 1)
                else:
                    em.inc("vector", lambda v, base=base:
                           v.tensor_mul(XB[:, base:base + 512], XB[:, base:base + 512],
                                        kbuf[:]), "v", 1)
        if idx + 2 < len(wseq):
            issue_wkv(idx + 2)

    em.wait("vector", "sc", max(sq_rel2.get(0, 0), sq_rel2.get(1, 0)))
    em.do("vector", lambda v: v.reduce_sum(nk2r[:], nk2[:, :].rearrange("p (i n) -> p i n", n=4),
                                           axis=AX.X))
    em.do("vector", lambda v: v.reduce_sum(nv2r[:], nv2[:, :].rearrange("p (i n) -> p i n", n=4),
                                           axis=AX.X))
    em.dr("vector")
    em.do("vector", lambda v: v.tensor_mul(nk2r[:], nk2r[:], nv2r[:]))
    em.dr("vector")
    vv = em.inc("vector", lambda v: v.reciprocal(nv2r[:], nk2r[:]), "v", 1)
    em.wait("scalar", "v", vv)
    sv = em.inc("scalar", lambda s: s.activation(invnf[:], nv2r[:], ACT.Sqrt), "sc", 1)
    em.wait("vector", "sc", sv)
    v_invn = em.inc("vector", lambda v: v.tensor_copy(invnb[:], invnf[:]), "v", 1)

    em.wait("tensor", "v", v_invn)
    em.wait("tensor", "v", psum_rel[4])
    v_amm = None
    for dc in range(KC):
        for i in range(TILES):
            fn = (lambda t, i=i, dc=dc:
                  t.matmul(ps[4][:, dc:dc + 1], XB[:, i * D + dc * 128:i * D + (dc + 1) * 128],
                           invnb[:, i:i + 1],
                           start=(i == 0), stop=(i == TILES - 1)))
            if i == TILES - 1 and dc == KC - 1:
                v_amm = em.inc("tensor", fn, "mm", 1)
            else:
                em.do("tensor", fn)
    em.wait("scalar", "mm", v_amm)
    v_at = em.inc("scalar", lambda s: s.activation(at_s[:], ps[4][:, 0:16], ACT.Copy,
                                                   bias=0.5, scale=0.5), "sc", 1)
    if debug:
        em.wait("sync", "sc", v_at)
        st(dbg["d_at"], at_s[:])
        em.wait("sync", "v", v_invn)
        st(dbg["d_invn"], invnf[:])

    p2_mm = em.val("mm")
    p2_v = em.val("v")
    p2_sc = em.val("sc")
    for cm in reversed(p2cm):
        cm.__exit__(None, None, None)
    for cm in reversed(wkv0cm):
        cm.__exit__(None, None, None)

    p3cm = []
    wqb = [sb_scoped("wqb0", [128, KC * 128], BF16, p3cm),
           sb_scoped("wqb1", [128, KC * 128], BF16, p3cm)]

    em.wait("gpsimd", "mm", p2_mm)
    em.wait("gpsimd", "v", p2_v)
    em.wait("gpsimd", "sc", p2_sc)

    v_wq = {}
    wq_rel = {}

    def issue_wq(qc):
        em.wait("gpsimd", "mm", wq_rel.get(qc - 2, 0))
        v_wq[qc] = ld(wqb[qc % 2][:].rearrange("p (k c) -> p k c", c=128),
                      win_h[:, qc * 128:(qc + 1) * 128].rearrange("(k p) c -> p k c", p=128),
                      eng="gpsimd")

    issue_wq(0)
    issue_wq(1)
    q_ps_rel = 0
    for qc in range(KC):
        em.wait("tensor", "w", v_wq[qc])
        em.wait("tensor", "sc", v_at)
        em.wait("tensor", "v", q_ps_rel)
        v_qmm = None
        for kc in range(KC):
            for tc in range(4):
                fn = (lambda t, qc=qc, kc=kc, tc=tc:
                      t.matmul(ps[tc][:], wqb[qc % 2][:, kc * 128:(kc + 1) * 128],
                               XA[:, kc * T + tc * 512: kc * T + (tc + 1) * 512],
                               start=(kc == 0), stop=(kc == KC - 1)))
                if kc == KC - 1 and tc == 3:
                    v_qmm = em.inc("tensor", fn, "mm", 1)
                else:
                    em.do("tensor", fn)
        wq_rel[qc] = v_qmm
        em.wait("vector", "mm", v_qmm)
        for tc in range(4):
            fn = (lambda v, qc=qc, tc=tc:
                  v.tensor_scalar(XB[:, qc * T + tc * 512: qc * T + (tc + 1) * 512],
                                  ps[tc][:], binq[:, qc:qc + 1], at_s[:, qc:qc + 1],
                                  ALU.add, ALU.mult))
            if tc == 3:
                q_ps_rel = em.inc("vector", fn, "v", 1)
            else:
                em.do("vector", fn)
        if qc + 2 < KC:
            issue_wq(qc + 2)

    p3_mm = em.val("mm")
    p3_v = em.val("v")
    for cm in reversed(p3cm):
        cm.__exit__(None, None, None)

    p4cm = []
    wob0b = sb_scoped("wob0b", [128, (KC - WO_SPL) * 512], BF16, p4cm)
    wob = [None, sb_scoped("wob1", [128, KC * 512], BF16, p4cm)]

    def wo0_slice(kc):
        buf, o = (wob0, kc) if kc < WO_SPL else (wob0b, kc - WO_SPL)
        return buf[:, o * 512:(o + 1) * 512]
    bob2 = sb_scoped("bob2", [128, 1024], F32, p4cm)
    xr2 = [sb_scoped("xr20", [128, 1024], F32, p4cm),
           sb_scoped("xr21", [128, 1024], F32, p4cm)]
    x2h = [sb_scoped("x2h0", [128, 1024], F32, p4cm),
           sb_scoped("x2h1", [128, 1024], F32, p4cm)]
    x2tl = [sb_scoped("x2tl0", [128, 1024], F32, p4cm),
            sb_scoped("x2tl1", [128, 1024], F32, p4cm)]
    xnb2_s = sb_scoped("xnb20", [128, D], BF16, p4cm)
    xnb2 = [xnb2_s, xnb2_s]
    stt4 = sb_scoped("stt4", [128, 24], F32, p4cm)

    em.wait("gpsimd", "mm", p3_mm)
    em.wait("gpsimd", "v", p3_v)
    em.wait("sync", "v", p3_v)

    o_ps_rel = dict(psum_rel)
    o_ps_rel[2] = p3_v
    o_ps_rel[3] = p3_v
    bob_rel = 0
    xr_rel = {0: 0, 1: 0}
    x2h_rel = {0: 0, 1: 0}
    x2st = {}
    x2tl_ld = {}
    x2tl_rel = {0: 0, 1: 0}
    tp4_evac = dict(tp_evac)
    xnb2_rel = {}
    rt_v_rel = [p3_v]
    smt_rel = [p3_v]
    smloc_done = {}
    v_zd = {}
    tpn4 = [0]
    wo_rel = 0
    v_wo = {}

    def p4b_ln(i):
        par = i % 2
        em.wait("vector", "ld", x2tl_ld[i])
        low = x2tl[par]
        high = x2h[par]
        em.dr("vector")
        for q in range(2):
            em.do("vector", lambda v, q=q: v.bn_stats(stt4[:, q * 6:(q + 1) * 6],
                                                      low[:, q * 512:(q + 1) * 512]))
        for q in range(2):
            em.do("vector", lambda v, q=q: v.bn_stats(stt4[:, 12 + q * 6:18 + q * 6],
                                                      high[:, q * 512:(q + 1) * 512]))
        em.dr("vector")
        va = em.inc("vector", lambda v: v.bn_aggr(csc[:, 0:2], stt4[:]), "v", 1)
        em.wait("scalar", "v", va)
        em.dr("scalar")
        em.do("scalar", lambda s: s.activation(csc[:, 3:4], csc[:, 1:2], ACT.Ln, bias=EPS))
        em.dr("scalar")
        sa = em.inc("scalar", lambda s: s.activation(csc[:, 4:5], csc[:, 3:4], ACT.Exp,
                                                     scale=-0.5), "sc", 1)
        em.wait("vector", "sc", sa)
        em.wait("vector", "mm", max(xnb2_rel.get(0, 0), xnb2_rel.get(1, 0)))
        em.dr("vector")
        em.do("vector", lambda v, p=par: v.tensor_scalar(xnb2[p][:, 0:1024], low[:],
                                                         csc[:, 0:1], csc[:, 4:5],
                                                         ALU.subtract, ALU.mult))
        v_z = em.inc("vector", lambda v, p=par:
                     v.tensor_scalar(xnb2[p][:, 1024:2048], high[:], csc[:, 0:1],
                                     csc[:, 4:5], ALU.subtract, ALU.mult), "v", 1)
        x2tl_rel[par] = v_z
        v_zd[i] = v_z

    def p4b_tp(i):
        par = i % 2
        em.wait("tensor", "v", v_zd[i])
        for kc in range(KC):
            slot = tpn4[0] % 2
            em.wait("tensor", "sc", tp4_evac[slot])
            v_tp = em.inc("tensor", lambda t, p=par, kc=kc, sl=slot:
                          t.transpose(psTp[sl][:, 0:128], xnb2[p][:, kc * 128:(kc + 1) * 128],
                                      identb[:]), "mm", 1)
            if kc == KC - 1:
                xnb2_rel[par] = v_tp
            em.wait("scalar", "mm", v_tp)
            tp4_evac[slot] = em.inc("scalar", lambda s, i=i, kc=kc, sl=slot:
                                    s.copy(XA[:, kc * T + i * 128: kc * T + (i + 1) * 128],
                                           psTp[sl][:, 0:128]), "sc", 1)
            tpn4[0] += 1
        em.dr("scalar")
        tile_ln2_done = em.inc("scalar", lambda s: s.nop(), "sc", 1)
        em.wait("tensor", "sc", tile_ln2_done)
        em.wait("tensor", "sc", v_at)
        em.wait("tensor", "w", v_rwb)
        em.wait("tensor", "v", rt_v_rel[0])
        v_rt = None
        for kc in range(KC):
            fn = (lambda t, kc=kc, i=i:
                  t.matmul(ps[0][:, 0:8], XA[:, kc * T + i * 128: kc * T + (i + 1) * 128],
                           rwb[:, kc * E:(kc + 1) * E],
                           start=(kc == 0), stop=(kc == KC - 1)))
            if kc == KC - 1:
                v_rt = em.inc("tensor", fn, "mm", 1)
            else:
                em.do("tensor", fn)
        em.wait("vector", "mm", v_rt)
        v_rel = em.inc("vector", lambda v: v.tensor_add(r8a[:], ps[0][:, 0:8], rbb[:]), "v", 1)
        rt_v_rel[0] = v_rel
        em.dr("vector")
        smloc_done[i] = em.inc("vector", lambda v, i=i:
                               v.tensor_add(smloc[:, i * E:(i + 1) * E], r8a[:],
                                            gum_sb[:, i * E:(i + 1) * E]), "v", 1)

    for h in range(2):
        em.wait("gpsimd", "mm", wo_rel)
        for j in range(2):
            n = 2 * h + j
            if j == 0:
                if h == 0:
                    vA = v_wo0_pref
                else:
                    vA = ld(wob0[:].rearrange("p (k c) -> p k c", c=512),
                            wout_h[0:WO_SPL * 128, n * 512:(n + 1) * 512]
                            .rearrange("(k p) c -> p k c", p=128), eng="gpsimd")
                vB = ld(wob0b[:].rearrange("p (k c) -> p k c", c=512),
                        wout_h[WO_SPL * 128:2048, n * 512:(n + 1) * 512]
                        .rearrange("(k p) c -> p k c", p=128), eng="gpsimd")
                v_wo[0] = max(vA, vB)
                continue
            vv = 0
            for q in range(2):
                vv = ld(wob[j][q * 64:(q + 1) * 64, :].rearrange("p (k c) -> p k c", c=512),
                        wout_h[:, n * 512:(n + 1) * 512]
                        .rearrange("(k p) c -> p k c", p=128)[q * 64:(q + 1) * 64],
                        eng="gpsimd")
            v_wo[j] = vv
        em.wait("sync", "v", bob_rel)
        v_bo = ld(bob2[:], bout_h[None, h * 1024:(h + 1) * 1024].broadcast_to((128, 1024)))
        if h == 1:
            em.wait("sync", "st", x2st[(0, 0)])
            x2tl_ld[0] = ld(x2tl[0][:], x2sp[0:128, 0:1024])
        for i in range(TILES):
            par = i % 2
            em.wait("sync", "v", xr_rel[par])
            v_xr = ld(xr2[par][:], x_h[i * 128:(i + 1) * 128, h * 1024:(h + 1) * 1024])
            ba, bb = (4, 5) if par == 0 else (2, 3)
            em.wait("tensor", "w", max(v_wo[0], v_wo[1]))
            em.wait("tensor", "v", p3_v)
            em.wait("tensor", "v", max(o_ps_rel[ba], o_ps_rel[bb]))
            v_omm = None
            for kc in range(KC):
                em.do("tensor", (lambda t, kc=kc, i=i, b=ba:
                                 t.matmul(ps[b][:], XB[:, kc * T + i * 128: kc * T + (i + 1) * 128],
                                          wo0_slice(kc),
                                          start=(kc == 0), stop=(kc == KC - 1))))
                fn = (lambda t, kc=kc, i=i, b=bb:
                      t.matmul(ps[b][:], XB[:, kc * T + i * 128: kc * T + (i + 1) * 128],
                               wob[1][:, kc * 512:(kc + 1) * 512],
                               start=(kc == 0), stop=(kc == KC - 1)))
                if kc == KC - 1:
                    v_omm = em.inc("tensor", fn, "mm", 1)
                else:
                    em.do("tensor", fn)
            if i == TILES - 1:
                wo_rel = v_omm
            em.wait("vector", "mm", v_omm)
            em.wait("vector", "ld", max(v_bo, v_xr))
            em.wait("vector", "st", x2h_rel[par])
            em.do("vector", lambda v, p=par, i=i:
                  v.scalar_tensor_tensor(x2h[p][:, 0:512], xr2[p][:, 0:512],
                                         c1t[:, i:i + 1], bob2[:, 0:512],
                                         ALU.mult, ALU.add))
            em.do("vector", lambda v, p=par, i=i:
                  v.scalar_tensor_tensor(x2h[p][:, 512:1024], xr2[p][:, 512:1024],
                                         c1t[:, i:i + 1], bob2[:, 512:1024],
                                         ALU.mult, ALU.add))
            em.dr("vector")
            v1 = em.inc("vector", lambda v, p=par, i=i, b=ba:
                        v.scalar_tensor_tensor(x2h[p][:, 0:512], ps[b][:],
                                               c2t[:, i:i + 1], x2h[p][:, 0:512],
                                               ALU.subtract, ALU.add), "v", 1)
            o_ps_rel[ba] = v1
            v2 = em.inc("vector", lambda v, p=par, i=i, b=bb:
                        v.scalar_tensor_tensor(x2h[p][:, 512:1024], ps[b][:],
                                               c2t[:, i:i + 1], x2h[p][:, 512:1024],
                                               ALU.subtract, ALU.add), "v", 1)
            o_ps_rel[bb] = v2
            xr_rel[par] = v2
            if i == TILES - 1:
                bob_rel = v2
            em.wait("sync", "v", v2)
            x2h_rel[par] = x2st[(h, i)] = st(
                x2sp[i * 128:(i + 1) * 128, h * 1024:(h + 1) * 1024], x2h[par][:])
            if h == 1:
                if i > 0:
                    p4b_tp(i - 1)
                p4b_ln(i)
                if i + 1 < TILES:
                    em.wait("sync", "v", x2tl_rel[(i + 1) % 2])
                    em.wait("sync", "st", x2st[(0, i + 1)])
                    x2tl_ld[i + 1] = ld(x2tl[(i + 1) % 2][:],
                                        x2sp[(i + 1) * 128:(i + 2) * 128, 0:1024])

    p4b_tp(TILES - 1)
    p4_pre_sc = em.val("sc")
    p4_pre_v = em.val("v")
    p4_pre_mm = em.val("mm")
    p4_mm = em.val("mm")
    p4_v = em.val("v")
    p4_sc = em.val("sc")
    if debug:
        em.wait("gpsimd", "sc", p4_sc)
        st(dbg["d_lnt2"], XA[:, 0:T], eng="gpsimd")
    for cm in reversed(p4cm):
        cm.__exit__(None, None, None)
    for cm in reversed(wob0cm):
        cm.__exit__(None, None, None)
    for cm in reversed(g4cm):
        cm.__exit__(None, None, None)

    p5cm = []
    wgub = [sb_scoped("wgub0", [128, KC * 256], BF16, p5cm),
            sb_scoped("wgub1", [128, KC * 256], BF16, p5cm)]
    wdb = [sb_scoped("wdb0", [128, NF * 128], BF16, p5cm),
           sb_scoped("wdb1", [128, NF * 128], BF16, p5cm)]
    XBE = sb_scoped("XBE", [128, NF * 1024 - KC * T], BF16, p5cm)
    gsc = sb_scoped("gsc", [128, 1024], BF16, p5cm)
    dsb = [sb_scoped("dsb0", [128, 1024], BF16, p5cm),
           sb_scoped("dsb1", [128, 1024], BF16, p5cm)]
    ysb = [sb_scoped(f"ysb{j}", [128, 128], F32, p5cm) for j in range(3)]
    x2c = [sb_scoped(f"x2c{j}", [128, 128], F32, p5cm) for j in range(3)]

    def hp_slice(f, lo, ln, fs):
        if f < 32:
            return XB[0:fs, f * 1024 + lo: f * 1024 + lo + ln]
        return XBE[0:fs, (f - 32) * 1024 + lo: (f - 32) * 1024 + lo + ln]
    bis = sb_scoped("bis", [128, 1024], F32, p5cm)
    bdump = sb_scoped("bdump", [128, 1024], F32, p5cm)
    smsc = sb_scoped("smsc", [128, 32], F32, p5cm)
    smtst = [sb_scoped("smtst0", [8, 128], F32, p5cm),
             sb_scoped("smtst1", [8, 128], F32, p5cm)]

    em.wait("gpsimd", "mm", p4_pre_mm)
    em.wait("gpsimd", "v", p4_pre_v)
    em.wait("gpsimd", "sc", p4_pre_sc)
    em.wait("sync", "v", p4_pre_v)

    gu_seq = [(gi, f) for gi in range(2) for f in range(NF)]
    v_wg = {}
    v_wu = {}
    wgu_rel = {}
    wgu_mid = {}

    def issue_wgu(k):
        gi, f = gu_seq[k]
        em.wait("gpsimd", "mm", wgu_mid.get(k - 2, 0))
        ld(wgub[k % 2][:, 0:2048].rearrange("p (f c) -> p f c", c=256),
           gu_h[f][:, 0:2048].rearrange("p (f c) -> p f c", c=256), eng="gpsimd")
        em.wait("gpsimd", "mm", wgu_rel.get(k - 2, 0))
        v_wg[k] = v_wu[k] = ld(wgub[k % 2][:, 2048:4096].rearrange("p (f c) -> p f c", c=256),
                               gu_h[f][:, 2048:4096].rearrange("p (f c) -> p f c", c=256),
                               eng="gpsimd")

    issue_wgu(0)
    issue_wgu(1)
    wd_seq = [(gi, dc) for gi in range(2) for dc in range(16)]
    v_wd = {}
    wd_rel = {}
    wd_mid = {}
    WDS = 22 * 128

    def issue_wd(k):
        gi, dc = wd_seq[k]
        em.wait("gpsimd", "mm", wd_mid.get(k - 2, 0))
        ld(wdb[k % 2][:, 0:WDS].rearrange("p (f c) -> p f c", c=128),
           wdi_h[dc][:, 0:WDS].rearrange("p (f c) -> p f c", c=128), eng="gpsimd")
        em.wait("gpsimd", "mm", wd_rel.get(k - 2, 0))
        v_wd[k] = ld(wdb[k % 2][:, WDS:].rearrange("p (f c) -> p f c", c=128),
                     wdi_h[dc][:, WDS:].rearrange("p (f c) -> p f c", c=128), eng="gpsimd")


    cc_done = [False]
    v_cc = None
    v_bis = 0
    v_ccin = 0

    def emit_burst_and_cc():
        nonlocal v_cc, v_bis, v_ccin
        em.dr("vector")
        em.do("vector", lambda v: v.reduce_max(smsc[:, 0:16],
                                               smloc[:, :].rearrange("p (i e) -> p i e", e=E),
                                               axis=AX.X))
        em.dr("vector")
        for i in range(TILES):
            em.do("vector", lambda v, i=i: v.tensor_single_scalar(smloc[:, i * E:(i + 1) * E],
                                                                  smloc[:, i * E:(i + 1) * E],
                                                                  smsc[:, i:i + 1], ALU.subtract))
        em.dr("vector")
        vm_b = em.inc("vector", lambda v: v.engine_nop(), "v", 1)
        em.wait("scalar", "v", vm_b)
        se_b = em.inc("scalar", lambda s: s.activation(smloc[:, :], smloc[:, :], ACT.Exp), "sc", 1)
        em.wait("vector", "sc", se_b)
        em.do("vector", lambda v: v.reduce_sum(smsc[:, 0:16],
                                               smloc[:, :].rearrange("p (i e) -> p i e", e=E),
                                               axis=AX.X))
        em.dr("vector")
        em.do("vector", lambda v: v.reciprocal(smsc[:, 16:32], smsc[:, 0:16]))
        em.dr("vector")
        for i in range(TILES):
            em.do("vector", lambda v, i=i: v.tensor_single_scalar(smloc[:, i * E:(i + 1) * E],
                                                                  smloc[:, i * E:(i + 1) * E],
                                                                  smsc[:, 16 + i:17 + i], ALU.mult))
        em.dr("vector")
        v_smdone = em.inc("vector", lambda v: v.engine_nop(), "v", 1)
        v_ccst = 0
        smtst_rel = {0: 0, 1: 0}
        smt_bank_rel = {0: rt_v_rel[0], 1: 0}
        for i in range(TILES):
            bk = i % 2
            em.wait("tensor", "v", max(v_smdone, smt_bank_rel[bk]))
            v_tp2 = em.inc("tensor", lambda t, i=i, bk=bk:
                           t.transpose(ps[bk][0:8, 0:128], smloc[:, i * E:(i + 1) * E],
                                       identf[:]), "mm", 1)
            em.wait("vector", "mm", v_tp2)
            em.wait("vector", "st", smtst_rel[bk])
            smt_bank_rel[bk] = em.inc("vector", lambda v, i=i, bk=bk:
                                      v.tensor_copy(smtst[bk][:], ps[bk][0:8, 0:128]),
                                      "v", 1)
            em.wait("sync", "v", smt_bank_rel[bk])
            smtst_rel[bk] = v_ccst = st(cc_in.ap()[0:8, i * 128:(i + 1) * 128],
                                        smtst[bk][:])
        smt_rel[0] = max(smt_bank_rel[0], smt_bank_rel[1])
        em.wait("gpsimd", "st", v_ccst)
        v_cc = em.inc("gpsimd", lambda g: g.collective_compute(
            "AllGather", ALU.bypass, replica_groups=[list(range(B))],
            ins=[cc_in.ap()], outs=[cc_out.ap()]), "g", 1)
        em.wait("gpsimd", "g", v_cc)
        for e in range(E):
            for h in range(2):
                vv = ld(bis[e * 16 + h * 8: e * 16 + h * 8 + 8, :],
                        cc_out.ap()[e::8, h * 1024:(h + 1) * 1024], eng="gpsimd")
        globals()
        nonlocal_set(vv)

    def nonlocal_set(vv):
        nonlocal v_bis
        v_bis = vv

    _bis_masks = [[(l ^ j) for l in range(32)] for j in (1, 2, 4, 8)]

    def emit_bis_iter(first=False):
        if first:
            em.wait("vector", "w", v_bis)
            em.wait("vector", "g", v_cc)
        em.dr("vector")
        em.do("vector", lambda v: v.tensor_add(bst[:, 2:3], bst[:, 0:1], bst[:, 1:2]))
        em.dr("vector")
        em.do("vector", lambda v: v.tensor_scalar(bdump[:], bis[:], bst[:, 2:3], None,
                                                  ALU.is_ge, ALU.add,
                                                  accum_out=bst[:, 4:5]))
        for mk in _bis_masks:
            em.dr("vector")
            em.do("vector", lambda v, mk=mk: v.stream_shuffle(bst[:, 5:6], bst[:, 4:5], mk))
            em.dr("vector")
            em.do("vector", lambda v: v.tensor_add(bst[:, 4:5], bst[:, 4:5], bst[:, 5:6]))
        em.dr("vector")
        em.do("vector", lambda v: v.tensor_scalar(bst[:, 6:7], bst[:, 4:5],
                                                  float(KSEL) - 0.5, bst[:, 1:2],
                                                  ALU.is_ge, ALU.mult))
        em.dr("vector")
        em.do("vector", lambda v: v.tensor_add(bst[:, 0:1], bst[:, 0:1], bst[:, 6:7]))
        em.do("vector", lambda v: v.tensor_scalar_mul(bst[:, 1:2], bst[:, 1:2], 0.5))

    def finish_bisection():
        em.dr("vector")
        return em.inc("vector", lambda v: v.engine_nop(), "v", 1)

    def emit_tau_and_scol(v_bisdone):
        em.wait("gpsimd", "v", v_bisdone)
        for e in range(E):
            st(tau_d[0:1, e:e + 1], bst[e * 16:e * 16 + 1, 0:1], eng="gpsimd")
        em.wait("sync", "gst", em.val("gst"))
        v_taub = ld(taub[:], tau_d[0:1, :].broadcast_to((128, E)))
        em.wait("vector", "ld", v_taub)
        for i in range(TILES):
            em.do("vector", lambda v, i=i: v.tensor_tensor(r8b[:], smloc[:, i * E:(i + 1) * E],
                                                           taub[:], ALU.is_ge))
            em.dr("vector")
            em.do("vector", lambda v, i=i: v.tensor_mul(r8b[:], r8b[:],
                                                        smloc[:, i * E:(i + 1) * E]))
            em.dr("vector")
            em.do("vector", lambda v, i=i: v.reduce_sum(scol[:, i:i + 1], r8b[:], axis=AX.X))
            em.dr("vector")
        vs = em.inc("vector", lambda v: v.engine_nop(), "v", 1)
        if debug:
            em.wait("sync", "v", vs)
            st(dbg["d_scol"], scol[:])
            st(dbg["d_sm"], smloc[:])
            st(dbg["d_tau"], taub[:])
        return vs

    bis_left = BIS_ITERS
    v_scol = 0
    dtp_n = [0]
    tp_v_rel = {0: 0, 1: 0}
    dsb_mm_rel = {0: 0, 1: 0}
    dn_sb_rel = {0: 0, 1: 0}
    x2c_rel = {j: 0 for j in range(3)}
    ysb_rel = {j: 0 for j in range(3)}
    init01 = max(rt_v_rel[0], smt_rel[0])
    pair_v_rel = {(0, 1): init01, (2, 3): max(o_ps_rel[2], o_ps_rel[3]),
                  (4, 5): max(o_ps_rel[4], o_ps_rel[5])}
    pair_sc_rel = {}
    silu_rel = {}
    dn_ps_rel = dict(o_ps_rel)
    x2r_rel = {0: 0, 1: 0}
    ysc_rel = {0: 0, 1: 0}
    bdbt_rel = 0
    gu_idx = 0
    wd_idx = 0
    for gi, (t0, nt) in enumerate(FFN_GROUPS):
        Tg = nt * 128
        og = t0 * 128
        tcs = [(0, 512)] + ([(512, Tg - 512)] if Tg > 512 else [])
        for f in range(NF):
            k = gu_idx
            fo, fs = FCH[f]
            PAIRS = ((0, 1), (2, 3), (4, 5))
            gp = PAIRS[(k + 1) % 3]
            upp = PAIRS[(k + 2) % 3]
            em.wait("tensor", "w", max(v_wg[k], v_wu[k]))
            em.wait("tensor", "sc", p4_pre_sc)
            em.wait("tensor", "sc", pair_sc_rel.get(gp, 0))
            em.wait("tensor", "v", pair_v_rel.get(gp, 0))
            for kc in range(KC):
                for ti, (tco, tcl) in enumerate(tcs):
                    em.do("tensor", (lambda t, kc=kc, k=k, fs=fs, ti=ti, tco=tco, tcl=tcl,
                                     og=og, gp=gp:
                                     t.matmul(ps[gp[ti]][0:fs, 0:tcl],
                                              wgub[k % 2][:, kc * 256:kc * 256 + fs],
                                              XA[:, kc * T + og + tco: kc * T + og + tco + tcl],
                                              start=(kc == 0), stop=(kc == KC - 1))))
            em.wait("tensor", "sc", pair_sc_rel.get(upp, 0))
            em.wait("tensor", "v", pair_v_rel.get(upp, 0))
            v_gu = None
            for kc in range(KC):
                for ti, (tco, tcl) in enumerate(tcs):
                    fn = (lambda t, kc=kc, k=k, fs=fs, ti=ti, tco=tco, tcl=tcl, og=og, upp=upp:
                          t.matmul(ps[upp[ti]][0:fs, 0:tcl],
                                   wgub[k % 2][:, kc * 256 + 128:kc * 256 + 128 + fs],
                                   XA[:, kc * T + og + tco: kc * T + og + tco + tcl],
                                   start=(kc == 0), stop=(kc == KC - 1)))
                    if kc == KC - 1 and ti == len(tcs) - 1:
                        v_gu = em.inc("tensor", fn, "mm", 1)
                    elif kc == 7 and ti == len(tcs) - 1:
                        wgu_mid[k] = em.inc("tensor", fn, "mm", 1)
                    else:
                        em.do("tensor", fn)
            wgu_rel[k] = v_gu
            em.wait("scalar", "mm", v_gu)
            for ti, (tco, tcl) in enumerate(tcs):
                fn = (lambda s, ti=ti, tco=tco, tcl=tcl, f=f, fs=fs, gp=gp:
                      s.activation(gsc[0:fs, tco:tco + tcl], ps[gp[ti]][0:fs, 0:tcl],
                                   ACT.Silu, bias=bgpp[0:fs, f:f + 1]))
                if ti == len(tcs) - 1:
                    pair_sc_rel[gp] = em.inc("scalar", fn, "sc", 1)
                else:
                    em.do("scalar", fn)
            em.wait("vector", "mm", v_gu)
            em.wait("vector", "sc", pair_sc_rel[gp])
            for ti, (tco, tcl) in enumerate(tcs):
                fn = (lambda v, ti=ti, tco=tco, tcl=tcl, f=f, fs=fs, upp=upp:
                      v.scalar_tensor_tensor(hp_slice(f, tco, tcl, fs),
                                             ps[upp[ti]][0:fs, 0:tcl], bupp[0:fs, f:f + 1],
                                             gsc[0:fs, tco:tco + tcl], ALU.add, ALU.mult))
                if ti == len(tcs) - 1:
                    pair_v_rel[upp] = em.inc("vector", fn, "v", 1)
                else:
                    em.do("vector", fn)
            gu_idx += 1
            if gu_idx + 1 < len(gu_seq):
                issue_wgu(gu_idx + 1)
            if gi == 0 and f == 3 and not cc_done[0]:
                emit_burst_and_cc()
                pair_v_rel[(0, 1)] = max(pair_v_rel.get((0, 1), 0), smt_rel[0])
                cc_done[0] = True
            if gi == 0 and f >= 6 and bis_left > 0:
                n_it = min(2, bis_left)
                for _ in range(n_it):
                    emit_bis_iter(first=(bis_left == BIS_ITERS))
                    bis_left -= 1
                if bis_left == 0:
                    v_bisdone = finish_bisection()
            if gi == 0 and f == 29 and not v_scol:
                v_scol = emit_tau_and_scol(v_bisdone)
            if gi == 0 and f == 36:
                issue_wd(0)
            if gi == 0 and f == 39:
                issue_wd(1)
        hp_done = max(pair_v_rel[(0, 1)] if (0, 1) in pair_v_rel else 0,
                      pair_v_rel.get((2, 3), 0), pair_v_rel.get((4, 5), 0))
        PAIRS = ((0, 1), (2, 3), (4, 5))

        def dn_evac_unit(dc, tc):
            gidx = t0 + tc
            dpar = dc % 2
            slot = dtp_n[0] % 2
            xs = dtp_n[0] % 3
            em.wait("tensor", "sc", dn_sb_rel[dpar])
            em.wait("tensor", "v", tp_v_rel[slot])
            v_tp = em.inc("tensor", lambda t, dp=dpar, tc=tc, sl=slot:
                          t.transpose(psTp[sl][:, 0:128],
                                      dsb[dp][:, tc * 128:(tc + 1) * 128],
                                      identb[:]), "mm", 1)
            if tc == 7:
                dsb_mm_rel[dpar] = v_tp
            em.wait("sync", "v", x2c_rel[xs])
            v_x2 = ld(x2c[xs][:],
                      x2sp[og + tc * 128: og + (tc + 1) * 128,
                           dc * 128:(dc + 1) * 128])
            em.wait("vector", "mm", v_tp)
            em.wait("vector", "ld", v_x2)
            em.wait("vector", "v", v_scol)
            em.wait("vector", "st", ysb_rel[xs])
            v_y = em.inc("vector", lambda v, sl=slot, xs=xs, gidx=gidx:
                         v.scalar_tensor_tensor(ysb[xs][:], psTp[sl][:, 0:128],
                                                scol[:, gidx:gidx + 1], x2c[xs][:],
                                                ALU.mult, ALU.add), "v", 1)
            tp_v_rel[slot] = v_y
            x2c_rel[xs] = v_y
            em.wait("sync", "v", v_y)
            ysb_rel[xs] = st(y_h[og + tc * 128: og + (tc + 1) * 128,
                                 dc * 128:(dc + 1) * 128], ysb[xs][:])
            dtp_n[0] += 1

        EVAC_AT = (3, 8, 13, 18, 23, 28, 33, 38)
        for dc in range(16):
            k = wd_idx
            pr = PAIRS[dc % 3]
            dpar = dc % 2
            em.wait("tensor", "w", v_wd[k])
            em.wait("tensor", "v", hp_done)
            em.wait("tensor", "v", pair_v_rel.get(pr, 0))
            em.wait("tensor", "sc", pair_sc_rel.get(pr, 0))
            v_dmm = None
            for f in range(NF):
                fo, fs = FCH[f]
                for half in range(2):
                    fn = (lambda t, f=f, fs=fs, k=k, b=pr[half], half=half:
                          t.matmul(ps[b][:, 0:512],
                                   wdb[k % 2][0:fs, f * 128:(f + 1) * 128],
                                   hp_slice(f, half * 512, 512, fs),
                                   start=(f == 0), stop=(f == NF - 1)))
                    if f == NF - 1 and half == 1:
                        v_dmm = em.inc("tensor", fn, "mm", 1)
                    elif f == 21 and half == 1:
                        wd_mid[k] = em.inc("tensor", fn, "mm", 1)
                    else:
                        em.do("tensor", fn)
                if dc > 0 and f in EVAC_AT:
                    dn_evac_unit(dc - 1, EVAC_AT.index(f))
            wd_rel[k] = v_dmm
            em.wait("scalar", "mm", v_dmm)
            em.wait("scalar", "mm", dsb_mm_rel[dpar])
            em.do("scalar", lambda s, b=pr[0], dp=dpar, dc=dc:
                  s.activation(dsb[dp][:, 0:512], ps[b][:, 0:512], ACT.Identity,
                               bias=bdcols[:, dc:dc + 1]))
            sB = em.inc("scalar", lambda s, b=pr[1], dp=dpar, dc=dc:
                        s.activation(dsb[dp][:, 512:1024], ps[b][:, 0:512], ACT.Identity,
                                     bias=bdcols[:, dc:dc + 1]), "sc", 1)
            dn_sb_rel[dpar] = sB
            pair_sc_rel[pr] = sB
            if k + 2 < len(wd_seq):
                issue_wd(k + 2)
            wd_idx += 1
        for tc in range(8):
            dn_evac_unit(15, tc)

    if debug:
        st(dbg["d_x2"], x2sp)
        em.wait("gpsimd", "sc", p4_sc)
        st(dbg["d_lnt"], XA[:, 0:T], eng="gpsimd")
    em.wait("gpsimd", "st", em.val("st"))
    em.wait("gpsimd", "gst", em.val("gst"))

    with nc.Block() as block:
        @block.sync
        def _(e):
            for fn in em.ops["sync"]:
                fn(e)

        @block.gpsimd
        def _(e):
            for fn in em.ops["gpsimd"]:
                fn(e)

        @block.tensor
        def _(e):
            for fn in em.ops["tensor"]:
                fn(e)

        @block.vector
        def _(e):
            for fn in em.ops["vector"]:
                fn(e)

        @block.scalar
        def _(e):
            for fn in em.ops["scalar"]:
                fn(e)

    for cm in reversed(p5cm):
        cm.__exit__(None, None, None)
    for cm in reversed(ctxs):
        cm.__exit__(None, None, None)
    return nc, dbg


_NC_CACHE = {}


def kernel(L0_attn_X, L0_attn_W_in, L0_attn_b_in, L0_attn_W_out, L0_attn_b_out,
           L0_attn_gamma, L0_attn_beta, L0_ffn_router_w, L0_ffn_router_b,
           L0_ffn_w_gate, L0_ffn_b_gate, L0_ffn_w_up, L0_ffn_b_up,
           L0_ffn_w_down, L0_ffn_b_down, _debug=False):
    key = ("prog", _debug)
    if key not in _NC_CACHE:
        _NC_CACHE[key] = build_program(debug=_debug)
    nc, dbg = _NC_CACHE[key]

    cos_t, sin_t = _rotary_tables()
    gum = _gumbel_table()
    wg32 = np.asarray(L0_ffn_w_gate, np.float32)
    wu32 = np.asarray(L0_ffn_w_up, np.float32)
    wd32 = np.asarray(L0_ffn_w_down, np.float32)
    gu_img = np.zeros((D, NF * 256), np.float32)
    wd_img = np.zeros((NF * 128, D), np.float32)
    KC_, NF_ = D // 128, NF
    for f, (fo, fs) in enumerate(FCH):
        gu_img[:, f * 256:f * 256 + fs] = wg32[:, fo:fo + fs]
        gu_img[:, f * 256 + 128:f * 256 + 128 + fs] = wu32[:, fo:fo + fs]
        wd_img[f * 128:f * 128 + fs, :] = wd32[fo:fo + fs, :]
    common = dict(
        win=np.ascontiguousarray(np.asarray(L0_attn_W_in, np.float32)),
        bin=np.ascontiguousarray(np.asarray(L0_attn_b_in, np.float32)),
        wout=np.ascontiguousarray(np.asarray(L0_attn_W_out, np.float32)),
        bout=np.ascontiguousarray(np.asarray(L0_attn_b_out, np.float32)),
        gamma=np.ascontiguousarray(np.asarray(L0_attn_gamma, np.float32)),
        beta=np.ascontiguousarray(np.asarray(L0_attn_beta, np.float32)),
        rw=np.ascontiguousarray(np.asarray(L0_ffn_router_w, np.float32)),
        rb=np.ascontiguousarray(np.asarray(L0_ffn_router_b, np.float32)),
        gu_img=gu_img.reshape(KC_, 128, NF_, 256).transpose(2, 1, 0, 3)
               .reshape(NF_, 128, KC_ * 256).copy(),
        bg=np.ascontiguousarray(np.asarray(L0_ffn_b_gate, np.float32)),
        bu=np.ascontiguousarray(np.asarray(L0_ffn_b_up, np.float32)),
        wd_img=wd_img.reshape(NF_, 128, 16, 128).transpose(2, 1, 0, 3)
               .reshape(16, 128, NF_ * 128).copy(),
        bd=np.ascontiguousarray(np.asarray(L0_ffn_b_down, np.float32)),
        cost=cos_t, sint=sin_t,
        identb=np.eye(128, dtype=BF), identf=np.eye(128, dtype=np.float32),
        cvec=np.array([0.0, 0.5, EPS, 1.0], dtype=np.float32),
    )
    X = np.asarray(L0_attn_X, dtype=np.float32)
    in_maps = []
    for b in range(B):
        m = dict(common)
        m["x"] = np.ascontiguousarray(X[b])
        m["gumb"] = np.ascontiguousarray(gum[b * T:(b + 1) * T])
        in_maps.append(m)

    res = run_bass_kernel_spmd(nc, in_maps, core_ids=list(range(B)))
    out = np.stack([r["y"] for r in res.results], axis=0).astype(np.float32)
    if _debug:
        return out, res
    return out


# version 20 (final)
# speedup vs baseline: 1.0526x; 1.0414x over previous
import numpy as np
import ml_dtypes
import concourse.bass as bass
import concourse.mybir as mybir
from concourse.bass_utils import run_bass_kernel_spmd

F32 = mybir.dt.float32
BF16 = mybir.dt.bfloat16
AX = mybir.AxisListType
ALU = mybir.AluOpType
ACT = mybir.ActivationFunctionType
BF = ml_dtypes.bfloat16

B, T, D, DFF, E = 8, 2048, 2048, 5461, 8
TILES = T // 128
KC = D // 128
ROT = 1024
NTOK = B * T
KSEL = NTOK // E
FCH = [(f * 128, min(128, DFF - f * 128)) for f in range((DFF + 127) // 128)]
NF = len(FCH)
FFN_GROUPS = [(0, 6), (6, 5), (11, 5)]
BIS_ITERS = 28
EPS = 1e-5


def _gumbel_table():
    def threefry(k0, k1, x0, x1):
        rot = [(13, 15, 26, 6), (17, 29, 16, 24)]
        ks = [np.uint32(k0), np.uint32(k1),
              np.uint32(k0) ^ np.uint32(k1) ^ np.uint32(0x1BD11BDA)]

        def rotl(x, r):
            r = np.uint32(r)
            return ((x << r) | (x >> (np.uint32(32) - r))).astype(np.uint32)

        with np.errstate(over="ignore"):
            x0 = (x0.astype(np.uint32) + ks[0]).astype(np.uint32)
            x1 = (x1.astype(np.uint32) + ks[1]).astype(np.uint32)
            for i in range(5):
                for r in rot[i % 2]:
                    x0 = (x0 + x1).astype(np.uint32)
                    x1 = (rotl(x1, r) ^ x0).astype(np.uint32)
                x0 = (x0 + ks[(i + 1) % 3]).astype(np.uint32)
                x1 = (x1 + ks[(i + 2) % 3] + np.uint32(i + 1)).astype(np.uint32)
        return x0, x1

    n = NTOK * E
    iota = np.arange(n, dtype=np.uint64)
    b1, b2 = threefry(0, 42, (iota >> np.uint64(32)).astype(np.uint32),
                      (iota & np.uint64(0xFFFFFFFF)).astype(np.uint32))
    bits = (b1 ^ b2).astype(np.uint32)
    u = ((bits >> np.uint32(9)) | np.uint32(0x3F800000)).view(np.float32) - np.float32(1.0)
    eps = np.float32(1e-10)
    g = -np.log((-np.log(u + eps, dtype=np.float32)) + eps, dtype=np.float32)
    return g.reshape(NTOK, E).astype(np.float32)


def _rotary_tables():
    inv_freq = 1.0 / (10000.0 ** (np.arange(0, ROT, 2, dtype=np.float64) / ROT))
    freqs = np.arange(T, dtype=np.float64)[:, None] * inv_freq[None, :]
    freqs = np.repeat(freqs, 2, axis=-1)
    return np.cos(freqs).astype(np.float32), np.sin(freqs).astype(np.float32)


class P:

    ENGS = ("sync", "gpsimd", "tensor", "vector", "scalar")

    def __init__(self):
        self.ops = {k: [] for k in self.ENGS}
        self.cnt = {}
        self.sems = {}
        self.lastwait = {k: {} for k in self.ENGS}

    def wait(self, eng, sname, val):
        if val is None or val <= 0:
            return
        if self.lastwait[eng].get(sname, -1) >= val:
            return
        self.lastwait[eng][sname] = val
        h = self.sems[sname]
        self.ops[eng].append(lambda e, h=h, v=val: e.wait_ge(h, v))

    def do(self, eng, fn):
        self.ops[eng].append(fn)

    def dr(self, eng):
        self.ops[eng].append(lambda e: e.drain())

    def inc(self, eng, fn, sname, by):
        v = self.cnt.get(sname, 0) + by
        self.cnt[sname] = v
        h = self.sems[sname]
        self.ops[eng].append(lambda e, fn=fn, h=h, by=by: fn(e).then_inc(h, by))
        return v

    def val(self, sname):
        return self.cnt.get(sname, 0)


def build_program(debug=False):
    nc = bass.Bass("TRN2", target_bir_lowering=False, detect_race_conditions=False)

    def din(name, shape, dt=F32):
        return nc.dram_tensor(name, list(shape), dt, kind="ExternalInput").ap()

    x_h = din("x", (T, D))
    win_h = din("win", (D, 3 * D))
    bin_h = din("bin", (3 * D,))
    wout_h = din("wout", (D, D))
    bout_h = din("bout", (D,))
    gam_h = din("gamma", (D,))
    bet_h = din("beta", (D,))
    rw_h = din("rw", (D, E))
    rb_h = din("rb", (E,))
    gu_h = din("gu_img", (D, NF * 256))
    bg_h = din("bg", (DFF,))
    bu_h = din("bu", (DFF,))
    wdi_h = din("wd_img", (NF * 128, D))
    bd_h = din("bd", (D,))
    cos_h = din("cost", (T, ROT))
    sin_h = din("sint", (T, ROT))
    gum_h = din("gumb", (128, TILES * E))
    idb_h = din("identb", (128, 128), BF16)
    idf_h = din("identf", (128, 128), F32)
    cst_h = din("cvec", (4,))

    y_h = nc.dram_tensor("y", [T, D], F32, kind="ExternalOutput").ap()
    x2sp = nc.dram_tensor("x2spill", [T, D], F32, kind="Internal").ap()
    tau_d = nc.dram_tensor("tau_d", [1, E], F32, kind="Internal").ap()
    cc_in = nc.dram_tensor("cc_in", [E, T], F32, kind="Internal")
    cc_out = nc.dram_tensor("cc_out", [B * E, T], F32, kind="Internal")

    dbg = {}
    if debug:
        for nm, shape in [("d_at", (128, KC)), ("d_sm", (128, TILES * E)),
                          ("d_tau", (128, E)), ("d_scol", (128, TILES)),
                          ("d_invn", (128, TILES)), ("d_x2", (T, D)),
                          ("d_lnt", (128, T)), ("d_lnt2", (128, T)),
                          ("d_x2t", (128, D)), ("d_xnb0", (128, D))]:
            dbg[nm] = nc.dram_tensor(nm, list(shape), F32, kind="ExternalOutput").ap()

    em = P()
    ctxs = []

    def sb(name, shape, dt):
        cm = nc.sbuf_tensor(name, list(shape), dt)
        h = cm.__enter__()
        ctxs.append(cm)
        return h

    def sb_scoped(name, shape, dt, bag):
        cm = nc.sbuf_tensor(name, list(shape), dt)
        h = cm.__enter__()
        bag.append(cm)
        return h

    def psum(name, shape, dt=F32):
        cm = nc.psum_tensor(name, list(shape), dt)
        h = cm.__enter__()
        ctxs.append(cm)
        return h

    for sname in ("ld", "st", "w", "gst", "g", "mm", "v", "sc"):
        cm = nc.semaphore("s_" + sname)
        em.sems[sname] = cm.__enter__()
        ctxs.append(cm)

    XA = sb("XA", [128, KC * T], BF16)
    XBW = max(KC * T, NF * 768)
    XB = sb("XB", [128, XBW], BF16)
    mu1 = sb("mu1", [128, TILES], F32)
    rstd1 = sb("rstd1", [128, TILES], F32)
    c1t = sb("c1t", [128, TILES], F32)
    c2t = sb("c2t", [128, TILES], F32)
    nk2 = sb("nk2", [128, TILES * 4], F32)
    nv2 = sb("nv2", [128, TILES * 4], F32)
    nk2r = sb("nk2r", [128, TILES], F32)
    nv2r = sb("nv2r", [128, TILES], F32)
    invnb = sb("invnb", [128, TILES], BF16)
    invnf = sb("invnf", [128, TILES], F32)
    at_s = sb("at_s", [128, KC], F32)
    binq = sb("binq", [128, KC], F32)
    smloc = sb("smloc", [128, TILES * E], F32)
    scol = sb("scol", [128, TILES], F32)
    taub = sb("taub", [128, E], F32)
    gum_sb = sb("gum_sb", [128, TILES * E], F32)
    rwb = sb("rwb", [128, KC * E], BF16)
    rbb = sb("rbb", [128, E], F32)
    identb = sb("identbs", [128, 128], BF16)
    identf = sb("identfs", [128, 128], F32)
    bgpp = sb("bgpp", [128, NF], F32)
    bupp = sb("bupp", [128, NF], F32)
    csc = sb("csc", [128, 16], F32)
    betc = sb("betc", [128, KC], F32)
    bst = sb("bst", [128, 8], F32)
    r8a = sb("r8a", [128, E], F32)
    r8b = sb("r8b", [128, E], F32)

    ps = [psum(f"ps{i}", [128, 512]) for i in range(6)]
    psTp = [psum("psTa", [128, 1024], BF16), psum("psTb", [128, 1024], BF16)]

    def ld(dst, src, eng="sync"):
        sname = "ld" if eng == "sync" else "w"
        return em.inc(eng, lambda e, d=dst, s=src: e.dma_start(out=d, in_=s), sname, 16)

    def st(dst, src, eng="sync"):
        sname = "st" if eng == "sync" else "gst"
        return em.inc(eng, lambda e, d=dst, s=src: e.dma_start(out=d, in_=s), sname, 16)

    stg = sb("stg", [128, 128], F32)
    cst3 = sb("cst3", [128, 4], F32)
    nc.const_aps.register(F32, 0.0, cst3[:, 0:1])
    nc.const_aps.register(F32, 0.5, cst3[:, 1:2])
    nc.const_aps.register(F32, EPS, cst3[:, 2:3])
    nc.const_aps.register(F32, 1.0, cst3[:, 3:4])
    ld(cst3[:], cst_h[None, :].broadcast_to((128, 4)))
    ld(rbb[:], rb_h[None, :].broadcast_to((128, E)))
    ld(identb[:], idb_h)
    ld(identf[:], idf_h)
    ld(gum_sb[:], gum_h)
    ld(bst[:, 0:2], cst_h[None, 0:2].broadcast_to((128, 2)))
    ld(bgpp[0:85, 42:43], bg_h[5376:DFF, None])
    ld(bupp[0:85, 42:43], bu_h[5376:DFF, None])
    v_rwb = ld(rwb[:].rearrange("p (k e) -> p k e", e=E),
               rw_h.rearrange("(k p) e -> p k e", p=128), eng="gpsimd")
    stp_rel = 0
    for dst, src_ap, rows in ((binq, bin_h[0:D], 16), (bgpp, bg_h[0:5376], 42),
                              (bupp, bu_h[0:5376], 42), (betc, bet_h[0:D], 16)):
        em.wait("sync", "sc", stp_rel)
        v_stg = ld(stg[0:rows, :], src_ap.rearrange("(q p) -> q p", p=128))
        em.wait("tensor", "ld", v_stg)
        v_tp0 = em.inc("tensor", lambda t, rows=rows:
                       t.transpose(ps[0][:, 0:rows], stg[0:rows, :], identf[0:rows, 0:rows]), "mm", 1)
        em.wait("scalar", "mm", v_tp0)
        stp_rel = em.inc("scalar", lambda s, dst=dst, rows=rows:
                         s.copy(dst[:, 0:rows], ps[0][:, 0:rows]), "sc", 1)
    setup_sc = em.val("sc")

    WO_SPL = 10
    wob0cm = []
    wob0 = sb_scoped("wob0", [128, WO_SPL * 512], BF16, wob0cm)
    wkv0cm = []
    wkv0 = sb_scoped("wkv0", [128, KC * 512], BF16, wkv0cm)

    p1cm = []
    xb3 = [sb_scoped(f"xb3_{j}", [128, D], F32, p1cm) for j in range(3)]
    gamt = sb_scoped("gamt", [128, D], BF16, p1cm)
    sqdump = sb_scoped("sqdump", [128, D], F32, p1cm)
    xnb = [sb_scoped("xnb0", [128, D], BF16, p1cm),
           sb_scoped("xnb1", [128, D], BF16, p1cm)]
    csc2 = sb_scoped("csc2", [128, 36], F32, p1cm)

    ld(gamt[:], gam_h[None, :].broadcast_to((128, D)), eng="gpsimd")
    v_gb = v_wk0 = ld(wkv0[:].rearrange("p (k c) -> p k c", c=512),
                      win_h[:, D:D + 512].rearrange("(k p) c -> p k c", p=128),
                      eng="gpsimd")

    tp_evac = {0: 0, 1: 0}
    xnb_rel = {}
    xb_rel = {0: 0, 1: 0, 2: 0}
    v_x = {}
    s0v = {}
    s1s = {}
    s2v = {}
    s3s = {}
    v_xnd = {}
    tpn = 0

    def cc(t, k):
        g = (t % 3) * 12
        return csc2[:, g + k:g + k + 1]

    def s0(t):
        em.wait("vector", "ld", v_x[t])
        em.wait("vector", "w", v_gb)
        em.do("vector", lambda v, t=t: v.reduce_sum(cc(t, 0), xb3[t % 3][:], axis=AX.X))
        em.dr("vector")
        em.do("vector", lambda v, t=t: v.tensor_scalar_mul(mu1[:, t:t + 1], cc(t, 0), 1.0 / D))
        s0v[t] = em.inc("vector", lambda v, t=t:
                        v.tensor_scalar_mul(cc(t, 2), cc(t, 0), -1.0 / D), "v", 1)

    def s1(t):
        em.wait("scalar", "v", s0v[t])
        em.do("scalar", lambda s, t=t: s.activation(sqdump[:], xb3[t % 3][:], ACT.Square,
                                                    bias=cc(t, 2), accum_out=cc(t, 3)))
        em.dr("scalar")
        s1s[t] = em.inc("scalar", lambda s, t=t: s.activation(cc(t, 4), cc(t, 3), ACT.Sqrt,
                                                              bias=EPS, scale=1.0 / D), "sc", 1)

    def s2(t):
        em.wait("vector", "sc", s1s[t])
        em.do("vector", lambda v, t=t: v.reciprocal(rstd1[:, t:t + 1], cc(t, 4)))
        em.dr("vector")
        em.do("vector", lambda v, t=t: v.tensor_scalar_add(c1t[:, t:t + 1], rstd1[:, t:t + 1], 1.0))
        em.do("vector", lambda v, t=t: v.tensor_mul(c2t[:, t:t + 1], mu1[:, t:t + 1],
                                                    rstd1[:, t:t + 1]))
        em.do("vector", lambda v, t=t: v.tensor_mul(cc(t, 5), rstd1[:, t:t + 1],
                                                    rstd1[:, t:t + 1]))
        em.dr("vector")
        s2v[t] = em.inc("vector", lambda v, t=t: v.tensor_mul(cc(t, 6), cc(t, 3), cc(t, 5)),
                        "v", 1)

    def s3(t):
        em.wait("scalar", "v", s2v[t])
        s3s[t] = em.inc("scalar", lambda s, t=t: s.activation(cc(t, 9), cc(t, 6), ACT.Sqrt,
                                                              bias=EPS, scale=1.0 / D), "sc", 1)

    def s4(t):
        par = t % 2
        em.wait("vector", "sc", s3s[t])
        em.wait("vector", "mm", xnb_rel.get(par, 0))
        em.do("vector", lambda v, t=t: v.reciprocal(cc(t, 10), cc(t, 9)))
        em.dr("vector")
        em.do("vector", lambda v, t=t: v.tensor_mul(cc(t, 11), cc(t, 10), rstd1[:, t:t + 1]))
        em.dr("vector")
        em.do("vector", lambda v, t=t: v.tensor_scalar(xb3[t % 3][:], xb3[t % 3][:],
                                                       mu1[:, t:t + 1], cc(t, 11),
                                                       ALU.subtract, ALU.mult))
        em.dr("vector")
        v_xn = em.inc("vector", lambda v, t=t, p=par: v.tensor_mul(xnb[p][:], xb3[t % 3][:],
                                                                   gamt[:]), "v", 1)
        xb_rel[t % 3] = v_xn
        v_xnd[t] = v_xn

    def s5(t):
        nonlocal_tpn = [0]
        par = t % 2
        em.wait("tensor", "v", v_xnd[t])
        for kc in range(KC):
            slot = (t * KC + kc) % 2
            em.wait("tensor", "sc", tp_evac[slot])
            v_tp = em.inc("tensor", lambda tt, p=par, kc=kc, sl=slot:
                          tt.transpose(psTp[sl][:, 0:128], xnb[p][:, kc * 128:(kc + 1) * 128],
                                       identb[:]), "mm", 1)
            if kc == KC - 1:
                xnb_rel[par] = v_tp
            em.wait("scalar", "mm", v_tp)
            tp_evac[slot] = em.inc("scalar", lambda s, t=t, kc=kc, sl=slot:
                                   s.activation(XA[:, kc * T + t * 128: kc * T + (t + 1) * 128],
                                                psTp[sl][:, 0:128], ACT.Identity,
                                                bias=betc[:, kc:kc + 1]), "sc", 1)

    for t in range(TILES + 2):
        if t < TILES:
            em.wait("sync", "v", xb_rel[t % 3])
            v_x[t] = ld(xb3[t % 3][:], x_h[t * 128:(t + 1) * 128, :])
            s0(t)
            s1(t)
        if 1 <= t <= TILES and t - 1 < TILES:
            s2(t - 1)
            s3(t - 1)
        if t >= 2:
            s4(t - 2)
            s5(t - 2)

    p1_mm = em.val("mm")
    p1_sc = em.val("sc")
    p1_v = em.val("v")
    for cm in reversed(p1cm):
        cm.__exit__(None, None, None)

    p2cm = []
    wkv = [wkv0, sb_scoped("wkv1", [128, KC * 512], BF16, p2cm)]
    cstl = [sb_scoped("cst0", [128, 512], F32, p2cm),
            sb_scoped("cst1", [128, 512], F32, p2cm)]
    sntl = [sb_scoped("snt0", [128, 512], F32, p2cm),
            sb_scoped("snt1", [128, 512], F32, p2cm)]
    bkt = [sb_scoped("bkt0", [128, 512], F32, p2cm),
           sb_scoped("bkt1", [128, 512], F32, p2cm)]
    kbuf2 = [sb_scoped("kbuf0", [128, 512], F32, p2cm),
             sb_scoped("kbuf1", [128, 512], F32, p2cm)]
    rs1 = sb_scoped("rs1", [128, 256], F32, p2cm)
    rs2 = sb_scoped("rs2", [128, 256], F32, p2cm)
    vrb = sb_scoped("vrb", [128, 512], F32, p2cm)
    sq5 = sb_scoped("sq5", [128, 512], F32, p2cm)

    em.wait("gpsimd", "mm", p1_mm)
    em.wait("gpsimd", "sc", p1_sc)
    em.wait("gpsimd", "v", p1_v)
    em.wait("sync", "sc", p1_sc)
    em.wait("sync", "v", p1_v)

    wseq = [("K", n) for n in range(4)] + [("V", n) for n in range(4)]
    v_wkv = {}
    wkv_rel = {}

    def issue_wkv(idx):
        kind, n = wseq[idx]
        colbase = D + (0 if kind == "K" else D) + n * 512
        em.wait("gpsimd", "mm", wkv_rel.get(idx - 2, 0))
        v_wkv[idx] = ld(wkv[idx % 2][:].rearrange("p (k c) -> p k c", c=512),
                        win_h[:, colbase:colbase + 512].rearrange("(k p) c -> p k c", p=128),
                        eng="gpsimd")

    v_wkv[0] = v_wk0
    issue_wkv(1)
    v_wo0_pref = ld(wob0[:].rearrange("p (k c) -> p k c", c=512),
                    wout_h[0:WO_SPL * 128, 0:512].rearrange("(k p) c -> p k c", p=128), eng="gpsimd")

    psum_rel = {4: 0, 5: 0}
    sq_rel2 = {}
    cs_rel = {0: 0, 1: 0}
    bkt_rel = {0: 0, 1: 0}

    for idx, (kind, n) in enumerate(wseq):
        colbase = D + (0 if kind == "K" else D) + n * 512
        em.wait("sync", "v", bkt_rel[n % 2])
        v_bk = ld(bkt[n % 2][:], bin_h[None, colbase:colbase + 512].broadcast_to((128, 512)))
        for i in range(TILES):
            v_cs = 0
            if n < 2:
                em.wait("sync", "v", cs_rel[i % 2])
                ld(cstl[i % 2][:], cos_h[i * 128:(i + 1) * 128, n * 512:(n + 1) * 512])
                v_cs = ld(sntl[i % 2][:], sin_h[i * 128:(i + 1) * 128, n * 512:(n + 1) * 512])
            bank = 4 + (idx * TILES + i) % 2
            em.wait("tensor", "w", v_wkv[idx])
            em.wait("tensor", "sc", p1_sc)
            em.wait("tensor", "v", psum_rel[bank])
            v_mm = None
            for kc in range(KC):
                fn = (lambda t, kc=kc, i=i, b=bank, p=idx % 2:
                      t.matmul(ps[b][:],
                               XA[:, kc * T + i * 128: kc * T + (i + 1) * 128],
                               wkv[p][:, kc * 512:(kc + 1) * 512],
                               start=(kc == 0), stop=(kc == KC - 1)))
                if kc == KC - 1:
                    v_mm = em.inc("tensor", fn, "mm", 1)
                else:
                    em.do("tensor", fn)
            if i == TILES - 1:
                wkv_rel[idx] = v_mm
            kpar = (idx * TILES + i) % 2
            kbuf = kbuf2[kpar]
            em.wait("vector", "mm", v_mm)
            em.wait("vector", "ld", v_bk)
            em.wait("vector", "sc", sq_rel2.get(kpar, 0))
            v_ev = em.inc("vector", lambda v, b=bank, p=n % 2, kb=kbuf:
                          v.tensor_add(kb[:], ps[b][:], bkt[p][:]), "v", 1)
            psum_rel[bank] = v_ev
            if i == TILES - 1:
                bkt_rel[n % 2] = v_ev
            nrm = nk2 if kind == "K" else nv2
            em.wait("scalar", "v", v_ev)
            sq_rel2[kpar] = em.inc("scalar", lambda s, nrm=nrm, i=i, n=n, kb=kbuf:
                                   s.activation(sq5[:], kb[:], ACT.Square,
                                                accum_out=nrm[:, i * 4 + n: i * 4 + n + 1]), "sc", 1)
            base = i * D + n * 512
            if n < 2:
                em.wait("vector", "ld", v_cs)
                ke, ko = kbuf[:, 0:512:2], kbuf[:, 1:512:2]
                ce, co = cstl[i % 2][:, 0:512:2], cstl[i % 2][:, 1:512:2]
                se, so = sntl[i % 2][:, 0:512:2], sntl[i % 2][:, 1:512:2]
                if kind == "K":
                    de = XB[:, base:base + 512:2]
                    do = XB[:, base + 1:base + 512:2]
                else:
                    de = vrb[:, 0:512:2]
                    do = vrb[:, 1:512:2]
                em.dr("vector")
                em.do("vector", lambda v, ke=ke, ce=ce: v.tensor_mul(rs1[:], ke, ce))
                em.do("vector", lambda v, ko=ko, se=se: v.tensor_mul(rs2[:], ko, se))
                em.dr("vector")
                em.do("vector", lambda v, de=de: v.tensor_sub(de, rs1[:], rs2[:]))
                em.do("vector", lambda v, ko=ko, co=co: v.tensor_mul(rs1[:], ko, co))
                em.do("vector", lambda v, ke=ke, so=so: v.tensor_mul(rs2[:], ke, so))
                em.dr("vector")
                vr = em.inc("vector", lambda v, do=do: v.tensor_add(do, rs1[:], rs2[:]), "v", 1)
                cs_rel[i % 2] = vr
                if kind == "V":
                    em.dr("vector")
                    em.inc("vector", lambda v, base=base:
                           v.tensor_mul(XB[:, base:base + 512], XB[:, base:base + 512],
                                        vrb[:]), "v", 1)
            else:
                em.dr("vector")
                if kind == "K":
                    em.inc("vector", lambda v, base=base:
                           v.tensor_copy(XB[:, base:base + 512], kbuf[:]), "v", 1)
                else:
                    em.inc("vector", lambda v, base=base:
                           v.tensor_mul(XB[:, base:base + 512], XB[:, base:base + 512],
                                        kbuf[:]), "v", 1)
        if idx + 2 < len(wseq):
            issue_wkv(idx + 2)

    em.wait("vector", "sc", max(sq_rel2.get(0, 0), sq_rel2.get(1, 0)))
    em.do("vector", lambda v: v.reduce_sum(nk2r[:], nk2[:, :].rearrange("p (i n) -> p i n", n=4),
                                           axis=AX.X))
    em.do("vector", lambda v: v.reduce_sum(nv2r[:], nv2[:, :].rearrange("p (i n) -> p i n", n=4),
                                           axis=AX.X))
    em.dr("vector")
    em.do("vector", lambda v: v.tensor_mul(nk2r[:], nk2r[:], nv2r[:]))
    em.dr("vector")
    vv = em.inc("vector", lambda v: v.reciprocal(nv2r[:], nk2r[:]), "v", 1)
    em.wait("scalar", "v", vv)
    sv = em.inc("scalar", lambda s: s.activation(invnf[:], nv2r[:], ACT.Sqrt), "sc", 1)
    em.wait("vector", "sc", sv)
    v_invn = em.inc("vector", lambda v: v.tensor_copy(invnb[:], invnf[:]), "v", 1)

    em.wait("tensor", "v", v_invn)
    em.wait("tensor", "v", psum_rel[4])
    v_amm = None
    for dc in range(KC):
        for i in range(TILES):
            fn = (lambda t, i=i, dc=dc:
                  t.matmul(ps[4][:, dc:dc + 1], XB[:, i * D + dc * 128:i * D + (dc + 1) * 128],
                           invnb[:, i:i + 1],
                           start=(i == 0), stop=(i == TILES - 1)))
            if i == TILES - 1 and dc == KC - 1:
                v_amm = em.inc("tensor", fn, "mm", 1)
            else:
                em.do("tensor", fn)
    em.wait("scalar", "mm", v_amm)
    v_at = em.inc("scalar", lambda s: s.activation(at_s[:], ps[4][:, 0:16], ACT.Copy,
                                                   bias=0.5, scale=0.5), "sc", 1)
    if debug:
        em.wait("sync", "sc", v_at)
        st(dbg["d_at"], at_s[:])
        em.wait("sync", "v", v_invn)
        st(dbg["d_invn"], invnf[:])

    p2_mm = em.val("mm")
    p2_v = em.val("v")
    p2_sc = em.val("sc")
    for cm in reversed(p2cm):
        cm.__exit__(None, None, None)
    for cm in reversed(wkv0cm):
        cm.__exit__(None, None, None)

    p3cm = []
    wqb = [sb_scoped("wqb0", [128, KC * 128], BF16, p3cm),
           sb_scoped("wqb1", [128, KC * 128], BF16, p3cm)]

    em.wait("gpsimd", "mm", p2_mm)
    em.wait("gpsimd", "v", p2_v)
    em.wait("gpsimd", "sc", p2_sc)

    v_wq = {}
    wq_rel = {}

    def issue_wq(qc):
        em.wait("gpsimd", "mm", wq_rel.get(qc - 2, 0))
        v_wq[qc] = ld(wqb[qc % 2][:].rearrange("p (k c) -> p k c", c=128),
                      win_h[:, qc * 128:(qc + 1) * 128].rearrange("(k p) c -> p k c", p=128),
                      eng="gpsimd")

    issue_wq(0)
    issue_wq(1)
    q_ps_rel = 0
    for qc in range(KC):
        em.wait("tensor", "w", v_wq[qc])
        em.wait("tensor", "sc", v_at)
        em.wait("tensor", "v", q_ps_rel)
        v_qmm = None
        for kc in range(KC):
            for tc in range(4):
                fn = (lambda t, qc=qc, kc=kc, tc=tc:
                      t.matmul(ps[tc][:], wqb[qc % 2][:, kc * 128:(kc + 1) * 128],
                               XA[:, kc * T + tc * 512: kc * T + (tc + 1) * 512],
                               start=(kc == 0), stop=(kc == KC - 1)))
                if kc == KC - 1 and tc == 3:
                    v_qmm = em.inc("tensor", fn, "mm", 1)
                else:
                    em.do("tensor", fn)
        wq_rel[qc] = v_qmm
        em.wait("vector", "mm", v_qmm)
        for tc in range(4):
            fn = (lambda v, qc=qc, tc=tc:
                  v.tensor_scalar(XB[:, qc * T + tc * 512: qc * T + (tc + 1) * 512],
                                  ps[tc][:], binq[:, qc:qc + 1], at_s[:, qc:qc + 1],
                                  ALU.add, ALU.mult))
            if tc == 3:
                q_ps_rel = em.inc("vector", fn, "v", 1)
            else:
                em.do("vector", fn)
        if qc + 2 < KC:
            issue_wq(qc + 2)

    p3_mm = em.val("mm")
    p3_v = em.val("v")
    for cm in reversed(p3cm):
        cm.__exit__(None, None, None)

    p4cm = []
    wob0b = sb_scoped("wob0b", [128, (KC - WO_SPL) * 512], BF16, p4cm)
    wob = [None, sb_scoped("wob1", [128, KC * 512], BF16, p4cm)]

    def wo0_slice(kc):
        buf, o = (wob0, kc) if kc < WO_SPL else (wob0b, kc - WO_SPL)
        return buf[:, o * 512:(o + 1) * 512]
    bob2 = sb_scoped("bob2", [128, 1024], F32, p4cm)
    xr2 = [sb_scoped("xr20", [128, 1024], F32, p4cm),
           sb_scoped("xr21", [128, 1024], F32, p4cm)]
    x2h = [sb_scoped("x2h0", [128, 1024], F32, p4cm),
           sb_scoped("x2h1", [128, 1024], F32, p4cm)]
    x2tl = [sb_scoped("x2tl0", [128, 1024], F32, p4cm),
            sb_scoped("x2tl1", [128, 1024], F32, p4cm)]
    xnb2_s = sb_scoped("xnb20", [128, D], BF16, p4cm)
    xnb2 = [xnb2_s, xnb2_s]
    stt4 = sb_scoped("stt4", [128, 24], F32, p4cm)

    em.wait("gpsimd", "mm", p3_mm)
    em.wait("gpsimd", "v", p3_v)
    em.wait("sync", "v", p3_v)

    o_ps_rel = dict(psum_rel)
    o_ps_rel[2] = p3_v
    o_ps_rel[3] = p3_v
    bob_rel = 0
    xr_rel = {0: 0, 1: 0}
    x2h_rel = {0: 0, 1: 0}
    x2st = {}
    x2tl_ld = {}
    x2tl_rel = {0: 0, 1: 0}
    tp4_evac = dict(tp_evac)
    xnb2_rel = {}
    rt_v_rel = [p3_v]
    smt_rel = [p3_v]
    smloc_done = {}
    v_zd = {}
    tpn4 = [0]
    wo_rel = 0
    v_wo = {}

    def p4b_ln(i):
        par = i % 2
        em.wait("vector", "ld", x2tl_ld[i])
        low = x2tl[par]
        high = x2h[par]
        em.dr("vector")
        for q in range(2):
            em.do("vector", lambda v, q=q: v.bn_stats(stt4[:, q * 6:(q + 1) * 6],
                                                      low[:, q * 512:(q + 1) * 512]))
        for q in range(2):
            em.do("vector", lambda v, q=q: v.bn_stats(stt4[:, 12 + q * 6:18 + q * 6],
                                                      high[:, q * 512:(q + 1) * 512]))
        em.dr("vector")
        va = em.inc("vector", lambda v: v.bn_aggr(csc[:, 0:2], stt4[:]), "v", 1)
        em.wait("scalar", "v", va)
        em.dr("scalar")
        em.do("scalar", lambda s: s.activation(csc[:, 3:4], csc[:, 1:2], ACT.Ln, bias=EPS))
        em.dr("scalar")
        sa = em.inc("scalar", lambda s: s.activation(csc[:, 4:5], csc[:, 3:4], ACT.Exp,
                                                     scale=-0.5), "sc", 1)
        em.wait("vector", "sc", sa)
        em.wait("vector", "mm", max(xnb2_rel.get(0, 0), xnb2_rel.get(1, 0)))
        em.dr("vector")
        em.do("vector", lambda v, p=par: v.tensor_scalar(xnb2[p][:, 0:1024], low[:],
                                                         csc[:, 0:1], csc[:, 4:5],
                                                         ALU.subtract, ALU.mult))
        v_z = em.inc("vector", lambda v, p=par:
                     v.tensor_scalar(xnb2[p][:, 1024:2048], high[:], csc[:, 0:1],
                                     csc[:, 4:5], ALU.subtract, ALU.mult), "v", 1)
        x2tl_rel[par] = v_z
        v_zd[i] = v_z

    def p4b_tp(i):
        par = i % 2
        em.wait("tensor", "v", v_zd[i])
        for kc in range(KC):
            slot = tpn4[0] % 2
            em.wait("tensor", "sc", tp4_evac[slot])
            v_tp = em.inc("tensor", lambda t, p=par, kc=kc, sl=slot:
                          t.transpose(psTp[sl][:, 0:128], xnb2[p][:, kc * 128:(kc + 1) * 128],
                                      identb[:]), "mm", 1)
            if kc == KC - 1:
                xnb2_rel[par] = v_tp
            em.wait("scalar", "mm", v_tp)
            tp4_evac[slot] = em.inc("scalar", lambda s, i=i, kc=kc, sl=slot:
                                    s.copy(XA[:, kc * T + i * 128: kc * T + (i + 1) * 128],
                                           psTp[sl][:, 0:128]), "sc", 1)
            tpn4[0] += 1
        em.dr("scalar")
        tile_ln2_done = em.inc("scalar", lambda s: s.nop(), "sc", 1)
        em.wait("tensor", "sc", tile_ln2_done)
        em.wait("tensor", "sc", v_at)
        em.wait("tensor", "w", v_rwb)
        em.wait("tensor", "v", rt_v_rel[0])
        v_rt = None
        for kc in range(KC):
            fn = (lambda t, kc=kc, i=i:
                  t.matmul(ps[0][:, 0:8], XA[:, kc * T + i * 128: kc * T + (i + 1) * 128],
                           rwb[:, kc * E:(kc + 1) * E],
                           start=(kc == 0), stop=(kc == KC - 1)))
            if kc == KC - 1:
                v_rt = em.inc("tensor", fn, "mm", 1)
            else:
                em.do("tensor", fn)
        em.wait("vector", "mm", v_rt)
        v_rel = em.inc("vector", lambda v: v.tensor_add(r8a[:], ps[0][:, 0:8], rbb[:]), "v", 1)
        rt_v_rel[0] = v_rel
        em.dr("vector")
        smloc_done[i] = em.inc("vector", lambda v, i=i:
                               v.tensor_add(smloc[:, i * E:(i + 1) * E], r8a[:],
                                            gum_sb[:, i * E:(i + 1) * E]), "v", 1)

    for h in range(2):
        em.wait("gpsimd", "mm", wo_rel)
        for j in range(2):
            n = 2 * h + j
            if j == 0:
                if h == 0:
                    vA = v_wo0_pref
                else:
                    vA = ld(wob0[:].rearrange("p (k c) -> p k c", c=512),
                            wout_h[0:WO_SPL * 128, n * 512:(n + 1) * 512]
                            .rearrange("(k p) c -> p k c", p=128), eng="gpsimd")
                vB = ld(wob0b[:].rearrange("p (k c) -> p k c", c=512),
                        wout_h[WO_SPL * 128:2048, n * 512:(n + 1) * 512]
                        .rearrange("(k p) c -> p k c", p=128), eng="gpsimd")
                v_wo[0] = max(vA, vB)
                continue
            v_wo[j] = ld(wob[j][:].rearrange("p (k c) -> p k c", c=512),
                         wout_h[:, n * 512:(n + 1) * 512].rearrange("(k p) c -> p k c", p=128),
                         eng="gpsimd")
        em.wait("sync", "v", bob_rel)
        v_bo = ld(bob2[:], bout_h[None, h * 1024:(h + 1) * 1024].broadcast_to((128, 1024)))
        if h == 1:
            em.wait("sync", "st", x2st[(0, 0)])
            x2tl_ld[0] = ld(x2tl[0][:], x2sp[0:128, 0:1024])
        for i in range(TILES):
            par = i % 2
            em.wait("sync", "v", xr_rel[par])
            v_xr = ld(xr2[par][:], x_h[i * 128:(i + 1) * 128, h * 1024:(h + 1) * 1024])
            ba, bb = (4, 5) if par == 0 else (2, 3)
            em.wait("tensor", "w", max(v_wo[0], v_wo[1]))
            em.wait("tensor", "v", p3_v)
            em.wait("tensor", "v", max(o_ps_rel[ba], o_ps_rel[bb]))
            v_omm = None
            for kc in range(KC):
                em.do("tensor", (lambda t, kc=kc, i=i, b=ba:
                                 t.matmul(ps[b][:], XB[:, kc * T + i * 128: kc * T + (i + 1) * 128],
                                          wo0_slice(kc),
                                          start=(kc == 0), stop=(kc == KC - 1))))
                fn = (lambda t, kc=kc, i=i, b=bb:
                      t.matmul(ps[b][:], XB[:, kc * T + i * 128: kc * T + (i + 1) * 128],
                               wob[1][:, kc * 512:(kc + 1) * 512],
                               start=(kc == 0), stop=(kc == KC - 1)))
                if kc == KC - 1:
                    v_omm = em.inc("tensor", fn, "mm", 1)
                else:
                    em.do("tensor", fn)
            if i == TILES - 1:
                wo_rel = v_omm
            em.wait("vector", "mm", v_omm)
            em.wait("vector", "ld", max(v_bo, v_xr))
            em.wait("vector", "st", x2h_rel[par])
            em.do("vector", lambda v, p=par, i=i:
                  v.scalar_tensor_tensor(x2h[p][:, 0:512], xr2[p][:, 0:512],
                                         c1t[:, i:i + 1], bob2[:, 0:512],
                                         ALU.mult, ALU.add))
            em.do("vector", lambda v, p=par, i=i:
                  v.scalar_tensor_tensor(x2h[p][:, 512:1024], xr2[p][:, 512:1024],
                                         c1t[:, i:i + 1], bob2[:, 512:1024],
                                         ALU.mult, ALU.add))
            em.dr("vector")
            v1 = em.inc("vector", lambda v, p=par, i=i, b=ba:
                        v.scalar_tensor_tensor(x2h[p][:, 0:512], ps[b][:],
                                               c2t[:, i:i + 1], x2h[p][:, 0:512],
                                               ALU.subtract, ALU.add), "v", 1)
            o_ps_rel[ba] = v1
            v2 = em.inc("vector", lambda v, p=par, i=i, b=bb:
                        v.scalar_tensor_tensor(x2h[p][:, 512:1024], ps[b][:],
                                               c2t[:, i:i + 1], x2h[p][:, 512:1024],
                                               ALU.subtract, ALU.add), "v", 1)
            o_ps_rel[bb] = v2
            xr_rel[par] = v2
            if i == TILES - 1:
                bob_rel = v2
            em.wait("sync", "v", v2)
            x2h_rel[par] = x2st[(h, i)] = st(
                x2sp[i * 128:(i + 1) * 128, h * 1024:(h + 1) * 1024], x2h[par][:])
            if h == 1:
                if i > 0:
                    p4b_tp(i - 1)
                p4b_ln(i)
                if i + 1 < TILES:
                    em.wait("sync", "v", x2tl_rel[(i + 1) % 2])
                    em.wait("sync", "st", x2st[(0, i + 1)])
                    x2tl_ld[i + 1] = ld(x2tl[(i + 1) % 2][:],
                                        x2sp[(i + 1) * 128:(i + 2) * 128, 0:1024])

    p4b_tp(TILES - 1)
    p4_pre_sc = em.val("sc")
    p4_pre_v = em.val("v")
    p4_pre_mm = em.val("mm")
    p4_mm = em.val("mm")
    p4_v = em.val("v")
    p4_sc = em.val("sc")
    if debug:
        em.wait("gpsimd", "sc", p4_sc)
        st(dbg["d_lnt2"], XA[:, 0:T], eng="gpsimd")
    for cm in reversed(p4cm):
        cm.__exit__(None, None, None)
    for cm in reversed(wob0cm):
        cm.__exit__(None, None, None)

    p5cm = []
    wgub = [sb_scoped("wgub0", [128, KC * 256], BF16, p5cm),
            sb_scoped("wgub1", [128, KC * 256], BF16, p5cm)]
    wdb = [sb_scoped("wdb0", [128, NF * 256], BF16, p5cm),
           sb_scoped("wdb1", [128, NF * 256], BF16, p5cm)]
    gsc = sb_scoped("gsc", [128, 768], BF16, p5cm)
    bdbt = sb_scoped("bdbt", [128, 256], F32, p5cm)
    x2r = [sb_scoped("x2r0", [128, 256], F32, p5cm),
           sb_scoped("x2r1", [128, 256], F32, p5cm)]
    ysc = [sb_scoped("ysc0", [128, 256], F32, p5cm),
           sb_scoped("ysc1", [128, 256], F32, p5cm)]
    bis = sb_scoped("bis", [128, 1024], F32, p5cm)
    bdump = sb_scoped("bdump", [128, 1024], F32, p5cm)
    smsc = sb_scoped("smsc", [128, 32], F32, p5cm)
    smtst = [sb_scoped("smtst0", [8, 128], F32, p5cm),
             sb_scoped("smtst1", [8, 128], F32, p5cm)]

    em.wait("gpsimd", "mm", p4_pre_mm)
    em.wait("gpsimd", "v", p4_pre_v)
    em.wait("gpsimd", "sc", p4_pre_sc)
    em.wait("sync", "v", p4_pre_v)

    gu_seq = [(gi, f) for gi in range(3) for f in range(NF)]
    v_wg = {}
    v_wu = {}
    wgu_rel = {}
    wgu_mid = {}

    def issue_wgu(k):
        gi, f = gu_seq[k]
        em.wait("gpsimd", "mm", wgu_mid.get(k - 2, 0))
        ld(wgub[k % 2][:, 0:2048].rearrange("p (k c) -> p k c", c=256),
           gu_h[0:1024, f * 256:(f + 1) * 256].rearrange("(k p) c -> p k c", p=128),
           eng="gpsimd")
        em.wait("gpsimd", "mm", wgu_rel.get(k - 2, 0))
        v_wg[k] = v_wu[k] = ld(
            wgub[k % 2][:, 2048:4096].rearrange("p (k c) -> p k c", c=256),
            gu_h[1024:2048, f * 256:(f + 1) * 256].rearrange("(k p) c -> p k c", p=128),
            eng="gpsimd")

    issue_wgu(0)
    issue_wgu(1)
    wd_seq = [(gi, dc) for gi in range(3) for dc in range(8)]
    v_wd = {}
    wd_rel = {}

    def issue_wd(k):
        gi, dc = wd_seq[k]
        em.wait("gpsimd", "mm", wd_rel.get(k - 2, 0))
        v_wd[k] = ld(wdb[k % 2][:].rearrange("p (f c) -> p f c", c=256),
                     wdi_h[:, dc * 256:(dc + 1) * 256].rearrange("(f p) c -> p f c", p=128),
                     eng="gpsimd")

    issue_wd(0)
    issue_wd(1)


    cc_done = [False]
    v_cc = None
    v_bis = 0
    v_ccin = 0

    def emit_burst_and_cc():
        nonlocal v_cc, v_bis, v_ccin
        em.dr("vector")
        em.do("vector", lambda v: v.reduce_max(smsc[:, 0:16],
                                               smloc[:, :].rearrange("p (i e) -> p i e", e=E),
                                               axis=AX.X))
        em.dr("vector")
        for i in range(TILES):
            em.do("vector", lambda v, i=i: v.tensor_single_scalar(smloc[:, i * E:(i + 1) * E],
                                                                  smloc[:, i * E:(i + 1) * E],
                                                                  smsc[:, i:i + 1], ALU.subtract))
        em.dr("vector")
        vm_b = em.inc("vector", lambda v: v.engine_nop(), "v", 1)
        em.wait("scalar", "v", vm_b)
        se_b = em.inc("scalar", lambda s: s.activation(smloc[:, :], smloc[:, :], ACT.Exp), "sc", 1)
        em.wait("vector", "sc", se_b)
        em.do("vector", lambda v: v.reduce_sum(smsc[:, 0:16],
                                               smloc[:, :].rearrange("p (i e) -> p i e", e=E),
                                               axis=AX.X))
        em.dr("vector")
        em.do("vector", lambda v: v.reciprocal(smsc[:, 16:32], smsc[:, 0:16]))
        em.dr("vector")
        for i in range(TILES):
            em.do("vector", lambda v, i=i: v.tensor_single_scalar(smloc[:, i * E:(i + 1) * E],
                                                                  smloc[:, i * E:(i + 1) * E],
                                                                  smsc[:, 16 + i:17 + i], ALU.mult))
        em.dr("vector")
        v_smdone = em.inc("vector", lambda v: v.engine_nop(), "v", 1)
        v_ccst = 0
        smtst_rel = {0: 0, 1: 0}
        smt_bank_rel = {0: rt_v_rel[0], 1: 0}
        for i in range(TILES):
            bk = i % 2
            em.wait("tensor", "v", max(v_smdone, smt_bank_rel[bk]))
            v_tp2 = em.inc("tensor", lambda t, i=i, bk=bk:
                           t.transpose(ps[bk][0:8, 0:128], smloc[:, i * E:(i + 1) * E],
                                       identf[:]), "mm", 1)
            em.wait("vector", "mm", v_tp2)
            em.wait("vector", "st", smtst_rel[bk])
            smt_bank_rel[bk] = em.inc("vector", lambda v, i=i, bk=bk:
                                      v.tensor_copy(smtst[bk][:], ps[bk][0:8, 0:128]),
                                      "v", 1)
            em.wait("sync", "v", smt_bank_rel[bk])
            smtst_rel[bk] = v_ccst = st(cc_in.ap()[0:8, i * 128:(i + 1) * 128],
                                        smtst[bk][:])
        smt_rel[0] = max(smt_bank_rel[0], smt_bank_rel[1])
        em.wait("gpsimd", "st", v_ccst)
        v_cc = em.inc("gpsimd", lambda g: g.collective_compute(
            "AllGather", ALU.bypass, replica_groups=[list(range(B))],
            ins=[cc_in.ap()], outs=[cc_out.ap()]), "g", 1)
        em.wait("gpsimd", "g", v_cc)
        for e in range(E):
            for h in range(2):
                vv = ld(bis[e * 16 + h * 8: e * 16 + h * 8 + 8, :],
                        cc_out.ap()[e::8, h * 1024:(h + 1) * 1024], eng="gpsimd")
        globals()
        nonlocal_set(vv)

    def nonlocal_set(vv):
        nonlocal v_bis
        v_bis = vv

    _bis_masks = [[(l ^ j) for l in range(32)] for j in (1, 2, 4, 8)]

    def emit_bis_iter(first=False):
        if first:
            em.wait("vector", "w", v_bis)
            em.wait("vector", "g", v_cc)
        em.dr("vector")
        em.do("vector", lambda v: v.tensor_add(bst[:, 2:3], bst[:, 0:1], bst[:, 1:2]))
        em.dr("vector")
        em.do("vector", lambda v: v.tensor_scalar(bdump[:], bis[:], bst[:, 2:3], None,
                                                  ALU.is_ge, ALU.add,
                                                  accum_out=bst[:, 4:5]))
        for mk in _bis_masks:
            em.dr("vector")
            em.do("vector", lambda v, mk=mk: v.stream_shuffle(bst[:, 5:6], bst[:, 4:5], mk))
            em.dr("vector")
            em.do("vector", lambda v: v.tensor_add(bst[:, 4:5], bst[:, 4:5], bst[:, 5:6]))
        em.dr("vector")
        em.do("vector", lambda v: v.tensor_scalar(bst[:, 6:7], bst[:, 4:5],
                                                  float(KSEL) - 0.5, bst[:, 1:2],
                                                  ALU.is_ge, ALU.mult))
        em.dr("vector")
        em.do("vector", lambda v: v.tensor_add(bst[:, 0:1], bst[:, 0:1], bst[:, 6:7]))
        em.do("vector", lambda v: v.tensor_scalar_mul(bst[:, 1:2], bst[:, 1:2], 0.5))

    def finish_bisection():
        em.dr("vector")
        return em.inc("vector", lambda v: v.engine_nop(), "v", 1)

    def emit_tau_and_scol(v_bisdone):
        em.wait("gpsimd", "v", v_bisdone)
        for e in range(E):
            st(tau_d[0:1, e:e + 1], bst[e * 16:e * 16 + 1, 0:1], eng="gpsimd")
        em.wait("sync", "gst", em.val("gst"))
        v_taub = ld(taub[:], tau_d[0:1, :].broadcast_to((128, E)))
        em.wait("vector", "ld", v_taub)
        for i in range(TILES):
            em.do("vector", lambda v, i=i: v.tensor_tensor(r8b[:], smloc[:, i * E:(i + 1) * E],
                                                           taub[:], ALU.is_ge))
            em.dr("vector")
            em.do("vector", lambda v, i=i: v.tensor_mul(r8b[:], r8b[:],
                                                        smloc[:, i * E:(i + 1) * E]))
            em.dr("vector")
            em.do("vector", lambda v, i=i: v.reduce_sum(scol[:, i:i + 1], r8b[:], axis=AX.X))
            em.dr("vector")
        vs = em.inc("vector", lambda v: v.engine_nop(), "v", 1)
        if debug:
            em.wait("sync", "v", vs)
            st(dbg["d_scol"], scol[:])
            st(dbg["d_sm"], smloc[:])
            st(dbg["d_tau"], taub[:])
        return vs

    bis_left = BIS_ITERS
    v_scol = 0
    init01 = max(rt_v_rel[0], smt_rel[0])
    pair_v_rel = {(0, 1): init01, (2, 3): max(o_ps_rel[2], o_ps_rel[3]),
                  (4, 5): max(o_ps_rel[4], o_ps_rel[5])}
    pair_sc_rel = {}
    silu_rel = {}
    dn_ps_rel = dict(o_ps_rel)
    x2r_rel = {0: 0, 1: 0}
    ysc_rel = {0: 0, 1: 0}
    bdbt_rel = 0
    gu_idx = 0
    wd_idx = 0
    for gi, (t0, nt) in enumerate(FFN_GROUPS):
        Tg = nt * 128
        og = t0 * 128
        tcs = [(0, 512)] + ([(512, Tg - 512)] if Tg > 512 else [])
        for f in range(NF):
            k = gu_idx
            fo, fs = FCH[f]
            PAIRS = ((0, 1), (2, 3), (4, 5))
            gp = PAIRS[(k + 1) % 3]
            upp = PAIRS[(k + 2) % 3]
            em.wait("tensor", "w", max(v_wg[k], v_wu[k]))
            em.wait("tensor", "sc", p4_pre_sc)
            em.wait("tensor", "sc", pair_sc_rel.get(gp, 0))
            em.wait("tensor", "v", pair_v_rel.get(gp, 0))
            for kc in range(KC):
                for ti, (tco, tcl) in enumerate(tcs):
                    em.do("tensor", (lambda t, kc=kc, k=k, fs=fs, ti=ti, tco=tco, tcl=tcl,
                                     og=og, gp=gp:
                                     t.matmul(ps[gp[ti]][0:fs, 0:tcl],
                                              wgub[k % 2][:, kc * 256:kc * 256 + fs],
                                              XA[:, kc * T + og + tco: kc * T + og + tco + tcl],
                                              start=(kc == 0), stop=(kc == KC - 1))))
            em.wait("tensor", "sc", pair_sc_rel.get(upp, 0))
            em.wait("tensor", "v", pair_v_rel.get(upp, 0))
            v_gu = None
            for kc in range(KC):
                for ti, (tco, tcl) in enumerate(tcs):
                    fn = (lambda t, kc=kc, k=k, fs=fs, ti=ti, tco=tco, tcl=tcl, og=og, upp=upp:
                          t.matmul(ps[upp[ti]][0:fs, 0:tcl],
                                   wgub[k % 2][:, kc * 256 + 128:kc * 256 + 128 + fs],
                                   XA[:, kc * T + og + tco: kc * T + og + tco + tcl],
                                   start=(kc == 0), stop=(kc == KC - 1)))
                    if kc == KC - 1 and ti == len(tcs) - 1:
                        v_gu = em.inc("tensor", fn, "mm", 1)
                    elif kc == 7 and ti == len(tcs) - 1:
                        wgu_mid[k] = em.inc("tensor", fn, "mm", 1)
                    else:
                        em.do("tensor", fn)
            wgu_rel[k] = v_gu
            em.wait("scalar", "mm", v_gu)
            for ti, (tco, tcl) in enumerate(tcs):
                fn = (lambda s, ti=ti, tco=tco, tcl=tcl, f=f, fs=fs, gp=gp:
                      s.activation(gsc[0:fs, tco:tco + tcl], ps[gp[ti]][0:fs, 0:tcl],
                                   ACT.Silu, bias=bgpp[0:fs, f:f + 1]))
                if ti == len(tcs) - 1:
                    pair_sc_rel[gp] = em.inc("scalar", fn, "sc", 1)
                else:
                    em.do("scalar", fn)
            em.wait("vector", "mm", v_gu)
            em.wait("vector", "sc", pair_sc_rel[gp])
            for ti, (tco, tcl) in enumerate(tcs):
                fn = (lambda v, ti=ti, tco=tco, tcl=tcl, f=f, fs=fs, Tg=Tg, upp=upp:
                      v.scalar_tensor_tensor(XB[0:fs, f * Tg + tco: f * Tg + tco + tcl],
                                             ps[upp[ti]][0:fs, 0:tcl], bupp[0:fs, f:f + 1],
                                             gsc[0:fs, tco:tco + tcl], ALU.add, ALU.mult))
                if ti == len(tcs) - 1:
                    pair_v_rel[upp] = em.inc("vector", fn, "v", 1)
                else:
                    em.do("vector", fn)
            gu_idx += 1
            if gu_idx + 1 < len(gu_seq):
                issue_wgu(gu_idx + 1)
            if gi == 0 and f == 3 and not cc_done[0]:
                emit_burst_and_cc()
                pair_v_rel[(0, 1)] = max(pair_v_rel.get((0, 1), 0), smt_rel[0])
                cc_done[0] = True
            if gi == 0 and f >= 6 and bis_left > 0:
                n_it = min(2, bis_left)
                for _ in range(n_it):
                    emit_bis_iter(first=(bis_left == BIS_ITERS))
                    bis_left -= 1
                if bis_left == 0:
                    v_bisdone = finish_bisection()
            if gi == 0 and f == 29 and not v_scol:
                v_scol = emit_tau_and_scol(v_bisdone)
        hp_done = max(pair_v_rel[(0, 1)] if (0, 1) in pair_v_rel else 0,
                      pair_v_rel.get((2, 3), 0), pair_v_rel.get((4, 5), 0))
        for dc in range(8):
            k = wd_idx
            em.wait("sync", "v", bdbt_rel)
            v_bd = ld(bdbt[:], bd_h[None, dc * 256:(dc + 1) * 256].broadcast_to((128, 256)))
            for ti in range(nt):
                gidx = t0 + ti
                bank = 4 + (k * nt + ti) % 2
                ypar = (k * nt + ti) % 2
                em.wait("tensor", "w", v_wd[k])
                em.wait("tensor", "v", hp_done)
                em.wait("tensor", "sc", pair_sc_rel.get((4, 5), 0))
                em.wait("tensor", "v", dn_ps_rel[bank])
                v_dmm = None
                for f in range(NF):
                    fo, fs = FCH[f]
                    fn = (lambda t, f=f, fs=fs, ti=ti, Tg=Tg, k=k, b=bank:
                          t.matmul(ps[b][:, 0:256],
                                   XB[0:fs, f * Tg + ti * 128: f * Tg + ti * 128 + 128],
                                   wdb[k % 2][0:fs, f * 256:(f + 1) * 256],
                                   start=(f == 0), stop=(f == NF - 1)))
                    if f == NF - 1:
                        v_dmm = em.inc("tensor", fn, "mm", 1)
                    else:
                        em.do("tensor", fn)
                if ti == nt - 1:
                    wd_rel[k] = v_dmm
                em.wait("sync", "v", x2r_rel[ypar])
                v_x2c = ld(x2r[ypar][:],
                           x2sp[gidx * 128:(gidx + 1) * 128, dc * 256:(dc + 1) * 256])
                em.wait("vector", "mm", v_dmm)
                em.wait("vector", "ld", max(v_bd, v_x2c))
                em.wait("vector", "v", v_scol)
                em.wait("vector", "st", ysc_rel[ypar])
                v1 = em.inc("vector", lambda v, b=bank, yp=ypar:
                            v.tensor_add(ysc[yp][:], ps[b][:, 0:256], bdbt[:]), "v", 1)
                dn_ps_rel[bank] = v1
                em.dr("vector")
                v2 = em.inc("vector", lambda v, yp=ypar, gidx=gidx:
                            v.scalar_tensor_tensor(ysc[yp][:], ysc[yp][:],
                                                   scol[:, gidx:gidx + 1], x2r[yp][:],
                                                   ALU.mult, ALU.add), "v", 1)
                x2r_rel[ypar] = v2
                if ti == nt - 1:
                    bdbt_rel = v2
                em.wait("sync", "v", v2)
                ysc_rel[ypar] = st(y_h[gidx * 128:(gidx + 1) * 128, dc * 256:(dc + 1) * 256],
                                   ysc[ypar][:])
            if k + 2 < len(wd_seq):
                issue_wd(k + 2)
            wd_idx += 1
        pair_v_rel[(4, 5)] = max(dn_ps_rel[4], dn_ps_rel[5])
        pair_sc_rel.pop((4, 5), None)

    if debug:
        st(dbg["d_x2"], x2sp)
        em.wait("gpsimd", "sc", p4_sc)
        st(dbg["d_lnt"], XA[:, 0:T], eng="gpsimd")
    em.wait("gpsimd", "st", em.val("st"))
    em.wait("gpsimd", "gst", em.val("gst"))

    with nc.Block() as block:
        @block.sync
        def _(e):
            for fn in em.ops["sync"]:
                fn(e)

        @block.gpsimd
        def _(e):
            for fn in em.ops["gpsimd"]:
                fn(e)

        @block.tensor
        def _(e):
            for fn in em.ops["tensor"]:
                fn(e)

        @block.vector
        def _(e):
            for fn in em.ops["vector"]:
                fn(e)

        @block.scalar
        def _(e):
            for fn in em.ops["scalar"]:
                fn(e)

    for cm in reversed(p5cm):
        cm.__exit__(None, None, None)
    for cm in reversed(ctxs):
        cm.__exit__(None, None, None)
    return nc, dbg


_NC_CACHE = {}


def kernel(L0_attn_X, L0_attn_W_in, L0_attn_b_in, L0_attn_W_out, L0_attn_b_out,
           L0_attn_gamma, L0_attn_beta, L0_ffn_router_w, L0_ffn_router_b,
           L0_ffn_w_gate, L0_ffn_b_gate, L0_ffn_w_up, L0_ffn_b_up,
           L0_ffn_w_down, L0_ffn_b_down, _debug=False):
    key = ("prog", _debug)
    if key not in _NC_CACHE:
        _NC_CACHE[key] = build_program(debug=_debug)
    nc, dbg = _NC_CACHE[key]

    cos_t, sin_t = _rotary_tables()
    gum = _gumbel_table()
    wg32 = np.asarray(L0_ffn_w_gate, np.float32)
    wu32 = np.asarray(L0_ffn_w_up, np.float32)
    wd32 = np.asarray(L0_ffn_w_down, np.float32)
    gu_img = np.zeros((D, NF * 256), np.float32)
    wd_img = np.zeros((NF * 128, D), np.float32)
    for f, (fo, fs) in enumerate(FCH):
        gu_img[:, f * 256:f * 256 + fs] = wg32[:, fo:fo + fs]
        gu_img[:, f * 256 + 128:f * 256 + 128 + fs] = wu32[:, fo:fo + fs]
        wd_img[f * 128:f * 128 + fs, :] = wd32[fo:fo + fs, :]
    common = dict(
        win=np.ascontiguousarray(np.asarray(L0_attn_W_in, np.float32)),
        bin=np.ascontiguousarray(np.asarray(L0_attn_b_in, np.float32)),
        wout=np.ascontiguousarray(np.asarray(L0_attn_W_out, np.float32)),
        bout=np.ascontiguousarray(np.asarray(L0_attn_b_out, np.float32)),
        gamma=np.ascontiguousarray(np.asarray(L0_attn_gamma, np.float32)),
        beta=np.ascontiguousarray(np.asarray(L0_attn_beta, np.float32)),
        rw=np.ascontiguousarray(np.asarray(L0_ffn_router_w, np.float32)),
        rb=np.ascontiguousarray(np.asarray(L0_ffn_router_b, np.float32)),
        gu_img=gu_img,
        bg=np.ascontiguousarray(np.asarray(L0_ffn_b_gate, np.float32)),
        bu=np.ascontiguousarray(np.asarray(L0_ffn_b_up, np.float32)),
        wd_img=wd_img,
        bd=np.ascontiguousarray(np.asarray(L0_ffn_b_down, np.float32)),
        cost=cos_t, sint=sin_t,
        identb=np.eye(128, dtype=BF), identf=np.eye(128, dtype=np.float32),
        cvec=np.array([0.0, 0.5, EPS, 1.0], dtype=np.float32),
    )
    X = np.asarray(L0_attn_X, dtype=np.float32)
    in_maps = []
    for b in range(B):
        m = dict(common)
        m["x"] = np.ascontiguousarray(X[b])
        m["gumb"] = np.ascontiguousarray(
            gum[b * T:(b + 1) * T].reshape(TILES, 128, E)
            .transpose(1, 0, 2).reshape(128, TILES * E))
        in_maps.append(m)

    res = run_bass_kernel_spmd(nc, in_maps, core_ids=list(range(B)))
    out = np.stack([r["y"] for r in res.results], axis=0).astype(np.float32)
    if _debug:
        return out, res
    return out
